# version 20
# baseline (speedup 1.0000x reference)
import sys
sys.path.insert(0, "/opt/trn_rl_repo")
from contextlib import ExitStack

import numpy as np
import ml_dtypes

import concourse.bass as bass
import concourse.mybir as mybir
from concourse.bass_utils import run_bass_kernel_spmd
from concourse.alu_op_type import AluOpType as OP

F32 = mybir.dt.float32
BF16 = mybir.dt.bfloat16
AF = mybir.ActivationFunctionType
AX = mybir.AxisListType.X

B, L, D = 4, 4096, 1024
TPC = 2048
TT = 256
Q = 128
CPT = TT // Q
H, HD, DS, DI = 32, 64, 128, 2048
CC = 18
DIP = 4384
DFF = 4096
NEG = -1.0e30
EPS = 1e-5

DBG = ()
DBGTILE = 8
GLEVEL = 9
SKIP = set()
STOP = "full"
NTILES = 16
NCORES = 8


RACECHECK = False


def build_nc():
    nc = bass.Bass("TRN2", target_bir_lowering=False,
                   detect_race_conditions=RACECHECK)
    dbgset = set(DBG)

    def din(name, shape):
        return nc.dram_tensor(name, list(shape), F32, kind="ExternalInput").ap()

    xcore = din("xcore", (TPC, D))
    xprev = din("xprev", (TPC, D))
    gate = din("gate", (128, 1))
    w_pj = din("w_pj", (D, DIP))
    w_zt = din("w_zt", (DI, D))
    w_ot = din("w_ot", (DI, D))
    w_fu = din("w_fu", (3 * DI, D))
    w_q1t = din("w_q1t", (D, DFF))
    w_q2t = din("w_q2t", (DFF, D))
    c_ut = din("c_ut", (128, 128))
    c_sl = din("c_sl", (128, 128))
    c_id = din("c_id", (128, 128))
    c_onesc = din("c_onesc", (128, 1))
    c_onesr = din("c_onesr", (1, 512))
    c_sel = din("c_sel", (128, 1))
    c_cwt = din("c_cwt", (CC * 128, 4))
    c_cb = din("c_cb", (128, CC))
    c_dtb = din("c_dtb", (32, 1))
    c_alog = din("c_alog", (32, 1))
    c_dbc = din("c_dbc", (128, H))
    c_b1c = din("c_b1c", (128, 32))
    c_b2bc = din("c_b2bc", (128, D))
    c_gnc = din("c_gnc", (128, 16))
    c_nwc = din("c_nwc", (128, 8))

    out = nc.dram_tensor("out", [TPC, D], F32, kind="ExternalOutput").ap()

    def scratch(name, shape, dt):
        kind = ("ExternalOutput" if (name in dbgset or name.replace("d_", "") in dbgset)
                else "Internal")
        return nc.dram_tensor(name, list(shape), dt, kind=kind).ap()

    s_wzc = scratch("s_wzc", (128, 8 * D), BF16)
    s_woc = scratch("s_woc", (128, 8 * D), BF16)
    s_g = scratch("s_g", (128, 16 * TPC), BF16)
    s_an = scratch("s_an", (128, 8 * TPC), BF16)
    s_h1 = scratch("s_h1", (16, 128, D), F32)
    s_hh = scratch("s_hh", (128, 32 * TPC), BF16)
    s_wq1 = scratch("s_wq1", (128, 8 * DFF), BF16)
    s_wq2 = scratch("s_wq2", (128, 32 * D), BF16)
    d_zx = scratch("d_zx", (128, 16 * TT), BF16) if "zx" in dbgset else None
    d_xbc = scratch("d_xbc", (128, CC * TT), BF16) if "xbc" in dbgset else None
    d_conv = scratch("d_conv", (128, CC * TT), BF16) if "conv" in dbgset else None
    d_dt = scratch("d_dt", (32, TT), F32) if "dt" in dbgset else None
    d_an2 = scratch("d_an2", (128, 8 * TT), BF16) if "an2" in dbgset else None
    d_ain = scratch("d_ain", (128, 1024), F32) if "ain" in dbgset else None
    d_stat = scratch("d_stat", (128, 16), F32) if "stat" in dbgset else None
    d_y = scratch("d_y", (128, 16 * TT), BF16) if "y" in dbgset else None

    ctx = ExitStack()
    with ctx:
        E = ctx.enter_context

        semh, semn = {}, {}
        for nm in ["tn", "vc", "sc", "gp",
                   "cg", "cy", "wz", "wf", "ws0", "ws1", "ql", "qs", "wp",
                   "x0", "x1", "bg", "dbx", "tw",
                   "tg0", "tg1", "ta0", "ta1", "tx0", "tx1", "th0", "th1",
                   "fw", "f0", "f1", "fh0", "fh1",
                   "g20", "g21", "h20", "h21", "o0", "o1"]:
            semh[nm] = E(nc.semaphore(nm))
            semn[nm] = 0
        waited = {}
        guards = {}

        class Eng:
            def __init__(self, raw, name):
                self.r = raw
                self.name = name

            def wait(self, snm, val=None):
                v = semn[snm] if val is None else val
                if v <= 0:
                    return
                key = (self.name, snm)
                if waited.get(key, 0) >= v:
                    return
                self.r.wait_ge(semh[snm], v)
                waited[key] = v

            def acq(self, region):
                for snm, val in guards.get(region, {}).items():
                    self.wait(snm, val)

            def __getattr__(self, a):
                fn = getattr(self.r, a)
                if callable(fn):
                    eng = self

                    def wrap(*args, **kw):
                        rr = fn(*args, **kw)
                        if hasattr(rr, "then_inc"):
                            eng.last = rr
                        return rr
                    return wrap
                return fn

        _inced = set()

        def inc(instr, snm, k=1):
            instr.then_inc(semh[snm], k)
            semn[snm] += k
            _inced.add(id(instr))
            return instr

        def incl(eng, snm, k=1):
            return inc(eng.last, snm, k)

        def fence(eng, snm):
            if id(eng.last) not in _inced:
                inc(eng.last, snm)
            eng.wait(snm)

        def consume(region, snm):
            d = guards.setdefault(region, {})
            d[snm] = semn[snm]

        ut_bf = E(nc.sbuf_tensor("ut_bf", [128, 128], BF16))
        sl_f = E(nc.sbuf_tensor("sl_f", [128, 128], F32))
        id_bf = E(nc.sbuf_tensor("id_bf", [128, 128], BF16))
        id_f = E(nc.sbuf_tensor("id_f", [128, 128], F32))
        onesc_bf = E(nc.sbuf_tensor("onesc_bf", [128, 1], BF16))
        onesc_f = E(nc.sbuf_tensor("onesc_f", [128, 1], F32))
        onesr_bf = E(nc.sbuf_tensor("onesr_bf", [1, 512], BF16))
        onesr_f = E(nc.sbuf_tensor("onesr_f", [1, 512], F32))
        sel_f = E(nc.sbuf_tensor("sel_f", [128, 1], F32))
        gate_f = E(nc.sbuf_tensor("gate_f", [128, 1], F32))
        cwt_f = E(nc.sbuf_tensor("cwt_f", [128, CC * 4], F32))
        cb_f = E(nc.sbuf_tensor("cb_f", [128, CC], F32))
        cwt_b = E(nc.sbuf_tensor("cwt_b", [128, CC * 4], BF16))
        cb_b = E(nc.sbuf_tensor("cb_b", [128, CC], BF16))
        dtb_f = E(nc.sbuf_tensor("dtb_f", [32, 1], F32))
        nega_f = E(nc.sbuf_tensor("nega_f", [32, 1], F32))
        dbc_f = E(nc.sbuf_tensor("dbc_f", [128, H], F32))
        b1c_f = E(nc.sbuf_tensor("b1c_f", [128, 32], F32))
        b2bc_f = E(nc.sbuf_tensor("b2bc_f", [128, D], F32))
        gnc_f = E(nc.sbuf_tensor("gnc_f", [128, 16], F32))
        nwc_f = E(nc.sbuf_tensor("nwc_f", [128, 8], F32))
        stat = E(nc.sbuf_tensor("stat", [128, 16], F32))
        xsums = E(nc.sbuf_tensor("xsums", [128, 8], F32))
        xsb = E(nc.sbuf_tensor("xsb", [128, 8], BF16))
        qcol = E(nc.sbuf_tensor("qcol", [128, 4], F32))
        thrc = E(nc.sbuf_tensor("thrc", [128, 4], F32))
        junkb = E(nc.sbuf_tensor("junkb", [128, 1024], BF16))
        qstg = E(nc.sbuf_tensor("qstg", [128, 1024], F32))
        qo1 = E(nc.sbuf_tensor("qo1", [128, 1024], BF16))
        qo2 = E(nc.sbuf_tensor("qo2", [128, 1024], BF16))

        pAB = E(nc.psum_tensor("pAB", [128, 1024], F32))
        pCD = E(nc.psum_tensor("pCD", [128, 1024], F32))
        pEF = E(nc.psum_tensor("pEF", [128, 1024], F32))
        pT = E(nc.psum_tensor("pT", [128, 1024], BF16))
        pS = E(nc.psum_tensor("pS", [128, 512], F32))

        block = E(nc.Block())

        @block.sync
        def _(_sy_raw):
            t = Eng(nc.tensor, "t")
            v = Eng(nc.vector, "v")
            s = Eng(nc.scalar, "s")
            g = Eng(nc.gpsimd, "g")
            sy = Eng(nc.sync, "y")

            def barrier(sems=None):
                if sems is None:
                    sems = list(semn.keys())
                for e in (t, v, s, g, sy):
                    for sn in sems:
                        e.wait(sn)

            def softsync(dma=False):
                sems = tuple(semn.keys()) if dma else ("tn", "vc", "sc", "gp")
                for e in (t, v, s, g):
                    for sn in sems:
                        e.wait(sn)

            for dst, srcap in [(ut_bf[:], c_ut), (id_bf[:], c_id),
                               (onesc_bf[:], c_onesc), (onesr_bf[:], c_onesr)]:
                g.wait("cg")
                inc(g.dma_start(dst, srcap), "cg", 16)
            for dst, srcap in [
                    (sl_f[:], c_sl), (id_f[:], c_id), (onesc_f[:], c_onesc),
                    (onesr_f[:], c_onesr), (sel_f[:], c_sel), (gate_f[:], gate),
                    (cwt_f.ap().rearrange("p (c k) -> p c k", c=CC),
                     c_cwt.rearrange("(c p) k -> p c k", p=128)),
                    (cb_f[:], c_cb), (dtb_f[:], c_dtb), (nega_f[:], c_alog),
                    (dbc_f[:], c_dbc), (b1c_f[:], c_b1c), (b2bc_f[:], c_b2bc),
                    (gnc_f[:], c_gnc), (nwc_f[:], c_nwc)]:
                sy.wait("cy")
                inc(sy.dma_start(dst, srcap), "cy", 16)
            g.wait("cg")
            inc(g.dma_start(cwt_b.ap().rearrange("p (c k) -> p c k", c=CC),
                            c_cwt.rearrange("(c p) k -> p c k", p=128)), "cg", 16)
            g.wait("cg")
            inc(g.dma_start(cb_b[:], c_cb), "cg", 16)

            s.wait("cy")
            inc(s.activation(nega_f[:], nega_f[:], AF.Exp), "sc")
            v.wait("sc")
            v.wait("cg")
            v.wait("cy")
            inc(v.tensor_scalar(nega_f[:], nega_f[:], -1.0, None, OP.mult), "vc")
            v.memset(xsums[:], 0.0)
            inc(v.memset(qcol[:], 0.0), "vc")

            if STOP == "consts":
                for snx in ["vc", "sc"]:
                    sy.wait(snx)
                return

            def qchunk(mi, c):
                if mi == 0:
                    srcr = w_q1t.rearrange("(g p) d -> p g d", p=128)
                    return srcr[:, c // 4:c // 4 + 1, (c % 4) * 1024:(c % 4 + 1) * 1024], c * 1024
                srcr = w_q2t.rearrange("(g p) d -> p g d", p=128)
                return srcr[:, c:c + 1, :], c * 1024

            gamma_chunks = [(mi, c) for mi in range(2) for c in range(32)]
            pass2_chunks = list(gamma_chunks)

            def emit_gamma(n):
                for _ in range(n):
                    if not gamma_chunks:
                        return
                    mi, c = gamma_chunks.pop(0)
                    srcq, _col = qchunk(mi, c)
                    g.wait("sc")
                    inc(g.dma_start(qstg.ap().rearrange("p (a d) -> p a d", a=srcq.shape[1]),
                                    srcq), "ql", 16)
                    s.wait("ql")
                    s.wait("vc")
                    inc(s.activation(qo1[:], qstg[:], AF.Abs, accum_out=qcol[:, 2:3]), "sc")
                    v.wait("sc")
                    inc(v.tensor_tensor(qcol[:, mi:mi + 1], qcol[:, mi:mi + 1],
                                        qcol[:, 2:3], OP.add), "vc")

            def emit_thr():
                t.wait("vc")
                t.acq("B7")
                for mi in range(2):
                    t.matmul(pS[0:1, mi:mi + 1], qcol[:, mi:mi + 1], onesc_f[:],
                             start=True, stop=True)
                incl(t, "tn")
                v.wait("tn")
                NW = float(DFF * D)
                inc(v.tensor_scalar(stat[0:1, 8:10], pS[0:1, 0:2], 0.5 / NW, 0.5 * EPS,
                                    OP.mult, OP.add), "vc")
                consume("B7", "vc")
                t.wait("vc")
                t.acq("B7")
                t.matmul(pS[:, 4:6], onesr_f[0:1, 0:128], stat[0:1, 8:10], start=True, stop=True)
                incl(t, "tn")
                v.wait("tn")
                v.tensor_copy(thrc[:, 0:2], pS[:, 4:6])
                fence(v, "vc")
                inc(v.tensor_scalar(thrc[:, 2:4], thrc[:, 0:2], -1.0, None, OP.mult), "vc")
                consume("B7", "vc")

            def emit_pass2(n):
                for _ in range(n):
                    if not pass2_chunks:
                        return
                    mi, c = pass2_chunks.pop(0)
                    srcq, col = qchunk(mi, c)
                    g.wait("vc")
                    g.wait("qs")
                    inc(g.dma_start(qstg.ap().rearrange("p (a d) -> p a d", a=srcq.shape[1]),
                                    srcq), "ql", 16)
                    v.wait("ql")
                    v.wait("qs")
                    v.tensor_scalar(qo1[:], qstg[:], thrc[:, mi:mi + 1], None, OP.is_gt)
                    inc(v.tensor_scalar(qo2[:], qstg[:], thrc[:, 2 + mi:3 + mi],
                                        None, OP.is_lt), "vc")
                    g.wait("vc")
                    inc(g.tensor_tensor(qo1[:], qo1[:], qo2[:], OP.subtract), "gp")
                    dst = s_wq1 if mi == 0 else s_wq2
                    inc(g.dma_start(dst[:, col:col + 1024], qo1[:]), "qs", 16)

            with ExitStack() as sctx:
                SE = sctx.enter_context
                zt = SE(nc.sbuf_tensor("zt", [128, 16 * D], BF16))
                fz = SE(nc.sbuf_tensor("fz", [128, 16 * D], BF16))
                wstg = SE(nc.sbuf_tensor("wstg", [128, 2 * D], BF16))

                for wi in range(2):
                    wt_src = w_zt if wi == 0 else w_ot
                    r0 = DI if wi == 0 else 2 * DI
                    s_dst = s_wzc if wi == 0 else s_woc
                    if wi == 1:
                        g.wait("tn")
                    inc(g.dma_start(zt.ap().rearrange("p (c i) -> p c i", c=16),
                                    wt_src.rearrange("(c p) i -> p c i", p=128)), "wz", 16)
                    inc(g.dma_start(fz.ap().rearrange("p (c o) -> p c o", c=16),
                                    w_fu[r0:r0 + DI].rearrange("(c p) o -> p c o", p=128)),
                        "wf", 16)
                    t.wait("wz")
                    t.wait("wf")
                    t.wait("vc")
                    if STOP == "wzload":
                        sy.wait("wz")
                        sy.wait("wf")
                        return
                    for ic in range(8):
                        ps = pAB if ic % 2 == 0 else pCD
                        rgs = ("B0", "B1") if ic % 2 == 0 else ("B2", "B3")
                        t.acq(rgs[0])
                        t.acq(rgs[1])
                        for jc in range(16):
                            lt = zt[:, jc * D + ic * 128: jc * D + (ic + 1) * 128]
                            t.matmul(ps[:, 0:512], lt, fz[:, jc * D: jc * D + 512],
                                     start=(jc == 0), stop=(jc == 15))
                            t.matmul(ps[:, 512:1024], lt, fz[:, jc * D + 512: jc * D + 1024],
                                     start=(jc == 0), stop=(jc == 15))
                        incl(t, "tn")
                        v.wait("tn")
                        v.acq("wstg%d" % (ic % 2))
                        wslot = wstg[:, (ic % 2) * D:(ic % 2) * D + D]
                        inc(v.tensor_scalar(wslot, ps[:, 0:1024], nwc_f[:, ic:ic + 1],
                                            None, OP.mult), "vc")
                        consume(rgs[0], "vc")
                        consume(rgs[1], "vc")
                        sy.wait("vc")
                        inc(sy.dma_start(s_dst[:, ic * D: ic * D + D], wslot),
                            "ws%d" % (ic % 2), 16)
                        consume("wstg%d" % (ic % 2), "ws%d" % (ic % 2))
                        if STOP == "wz1":
                            sy.wait("ws0")
                            return
                startup_done = dict(semn)

            if STOP == "startup":
                for snx in ["tn", "vc", "sc", "gp", "o0", "o1", "qs", "bg", "dbx", "th0", "th1", "fh0", "fh1", "ws0", "ws1"]:
                    sy.wait(snx)
                return

            with ExitStack() as actx:
                AE = actx.enter_context
                wpj = AE(nc.sbuf_tensor("wpj", [128, 8 * DIP], BF16))
                xt = [AE(nc.sbuf_tensor("xt%d" % i, [128, 2048], F32)) for i in range(2)]
                ain = AE(nc.sbuf_tensor("ain", [128, 1024], F32))
                anstg = AE(nc.sbuf_tensor("anstg", [128, 1024], BF16))
                an = AE(nc.sbuf_tensor("an", [128, 8 * TT], BF16))
                sz = AE(nc.sbuf_tensor("sz", [128, 16 * TT], BF16))
                xbps = [AE(nc.sbuf_tensor("xbp%d" % i, [128, CC * (TT + 3)], BF16))
                        for i in range(2)]
                xcv = AE(nc.sbuf_tensor("xcv", [128, CC * TT], BF16))
                halob = AE(nc.sbuf_tensor("halob", [128, CC * 3], BF16))
                cacc = [AE(nc.sbuf_tensor("cacc%d" % i, [128, TT], BF16)) for i in range(6)]
                dtraws = [AE(nc.sbuf_tensor("dtraw%d" % i, [32, TT], F32)) for i in range(2)]
                dtt1 = AE(nc.sbuf_tensor("dtt1", [32, TT], F32))
                dtt2 = AE(nc.sbuf_tensor("dtt2", [32, TT], F32))
                dtsp = AE(nc.sbuf_tensor("dtsp", [32, TT], F32))
                ldab = AE(nc.sbuf_tensor("ldab", [32, TT], BF16))
                dtT = AE(nc.sbuf_tensor("dtT", [128, 2 * 32], F32))
                dtTb = AE(nc.sbuf_tensor("dtTb", [128, 2 * 32], BF16))
                ldatm = AE(nc.sbuf_tensor("ldatm", [128, 2 * 32], BF16))
                negca = AE(nc.sbuf_tensor("negca", [128, 32], F32))
                endrow = AE(nc.sbuf_tensor("endrow", [1, 32], F32))
                esc = AE(nc.sbuf_tensor("esc", [128, 32], F32))
                escdt = AE(nc.sbuf_tensor("escdt", [128, 32], BF16))
                decbc = AE(nc.sbuf_tensor("decbc", [128, 32], F32))
                btm = AE(nc.sbuf_tensor("btm", [128, 128], BF16))
                xtm = AE(nc.sbuf_tensor("xtm", [128, 2048], BF16))
                xti = AE(nc.sbuf_tensor("xti", [128, 2048], BF16))
                xtp = AE(nc.sbuf_tensor("xtp", [128, 2048], BF16))
                mpre = AE(nc.sbuf_tensor("mpre", [128, 1024], F32))
                mtb = AE(nc.sbuf_tensor("mtb", [128, 1024], BF16))
                eca = AE(nc.sbuf_tensor("eca", [128, 1024], BF16))
                gmt = AE(nc.sbuf_tensor("gmt", [128, 1024], BF16))
                cdec = AE(nc.sbuf_tensor("cdec", [128, 1024], BF16))
                gbb = AE(nc.sbuf_tensor("gbb", [128, 128], BF16))
                Sf = AE(nc.sbuf_tensor("Sf", [128, 2048], F32))
                Sbb = AE(nc.sbuf_tensor("Sbb", [128, 2048], BF16))
                yb = AE(nc.sbuf_tensor("yb", [128, 16 * TT], BF16))
                rrow = AE(nc.sbuf_tensor("rrow", [1, TT], F32))
                rrowb = AE(nc.sbuf_tensor("rrowb", [1, TT], BF16))
                rsb = AE(nc.sbuf_tensor("rsb", [128, TT], BF16))

                barrier()

                inc(g.dma_start(wpj.ap().rearrange("p (c m) -> p c m", c=8),
                                w_pj.rearrange("(c p) m -> p c m", p=128)), "wp", 16)
                v.wait("wp")
                for dc in range(8):
                    ins = v.tensor_scalar(wpj[:, dc * DIP:(dc + 1) * DIP],
                                          wpj[:, dc * DIP:(dc + 1) * DIP],
                                          nwc_f[:, dc:dc + 1], None, OP.mult)
                inc(ins, "vc")
                wpj_ready = semn["vc"]
                v.memset(Sf[:], 0.0)
                inc(v.memset(Sbb[:], 0.0), "vc")

                def ln_stats(xin, want_rsn, cb=0):
                    C = lambda n: stat[:, cb + n:cb + n + 1]
                    s.wait("vc")
                    s.activation(junkb[:], xin, AF.Identity, accum_out=C(0))
                    fence(s, "sc")
                    inc(s.activation(junkb[:], xin, AF.Square, accum_out=C(1)), "sc")
                    v.wait("sc")
                    v.tensor_scalar(C(2), C(0), 1.0 / D, None, OP.mult)
                    v.tensor_scalar(C(3), C(1), -1.0 / D, None, OP.mult)
                    fence(v, "vc")
                    v.scalar_tensor_tensor(C(3), C(2), C(2),
                                           C(3), OP.mult, OP.add)
                    fence(v, "vc")
                    inc(v.tensor_scalar(C(4), C(3), -1.0, EPS,
                                        OP.mult, OP.add), "vc")
                    s.wait("vc")
                    s.activation(C(5), C(4), AF.Ln)
                    fence(s, "sc")
                    inc(s.activation(C(5), C(5), AF.Exp, scale=-0.5), "sc")
                    v.wait("sc")
                    v.tensor_scalar(C(6), C(2), -1.0, None, OP.mult)
                    fence(v, "vc")
                    ins2 = v.tensor_tensor(C(6), C(6), C(5), OP.mult)
                    if not want_rsn:
                        inc(ins2, "vc")
                        v.wait("vc")
                        return
                    fence(v, "vc")
                    v.reciprocal(C(7), C(4))
                    v.tensor_scalar(C(3), C(3), -1.0, None, OP.mult)
                    fence(v, "vc")
                    v.tensor_tensor(C(7), C(7), C(3), OP.mult)
                    fence(v, "vc")
                    inc(v.tensor_scalar(C(7), C(7), 1.0, EPS,
                                        OP.mult, OP.add), "vc")
                    s.wait("vc")
                    s.activation(C(7), C(7), AF.Ln)
                    fence(s, "sc")
                    inc(s.activation(C(7), C(7), AF.Exp, scale=-0.5), "sc")
                    v.wait("sc")

                mslots = [(pAB, 0, "B0"), (pAB, 512, "B1"), (pCD, 0, "B2"), (pCD, 512, "B3")]
                thr_emitted = False
                an_ready_map = {}
                sbb_sc = [0]

                pend = []
                pend_k = [None]
                pend_an = [0]
                feed_rot = [0]

                def feed(n=1, allslots=True):
                    for _ in range(n):
                        if not pend:
                            return
                        mc = pend.pop(0)
                        kk = pend_k[0]
                        si = feed_rot[0] % 4
                        feed_rot[0] += 1
                        ps, off, rg = mslots[si]
                        t.acq(rg)
                        t.wait("vc", wpj_ready)
                        t.wait("vc", pend_an[0])
                        mw = 32 if mc == 34 else 128
                        for dc in range(8):
                            t.matmul(ps[0:mw, off:off + TT],
                                     wpj[:, dc * DIP + mc * 128: dc * DIP + mc * 128 + mw],
                                     an[:, dc * TT:(dc + 1) * TT],
                                     start=(dc == 0), stop=(dc == 7))
                        incl(t, "tn")
                        s.wait("tn")
                        if mc < 34:
                            cc = mc - 16
                            s.acq("xbp%d" % (kk % 2))
                            inc(s.copy(
                                xbps[kk % 2][:, cc * (TT + 3) + 3: cc * (TT + 3) + 3 + TT],
                                ps[0:128, off:off + TT]), "sc")
                        else:
                            s.acq("dtraw%d" % (kk % 2))
                            inc(s.copy(dtraws[kk % 2][:], ps[0:32, off:off + TT]),
                                "sc")
                        consume(rg, "sc")

                xld_map = {}

                def emit_xload(k):
                    srcx = xcore if k >= 8 else xprev
                    trow2 = (k % 8) * TT
                    xbuf2 = xt[k % 2]
                    sy.acq("xt%d" % (k % 2))
                    xsem = "x%d" % (k % 2)
                    inc(sy.dma_start(xbuf2.ap().rearrange("p (s d) -> p s d", s=2),
                                     srcx[trow2:trow2 + TT].rearrange("(s p) d -> p s d",
                                                                      p=128)), xsem, 16)
                    xld_map[k] = semn[xsem]

                def emit_ln(k):
                    xbuf2 = xt[k % 2]
                    xsem = "x%d" % (k % 2)
                    xld = xld_map[k]
                    for j in range(2):
                        v.wait(xsem, xld)
                        s.wait(xsem, xld)
                        xsub = xbuf2[:, j * 1024:(j + 1) * 1024]
                        ln_stats(xsub, True)
                        v.acq("ain_r")
                        v.tensor_scalar(ain[:, 0:1024], xsub,
                                        stat[:, 5:6], stat[:, 6:7], OP.mult, OP.add)
                        fence(v, "vc")
                        if k == DBGTILE and j == 0 and d_stat is not None:
                            sy.wait("vc")
                            inc(sy.dma_start(d_stat, stat[:]), "dbx", 16)
                            consume("statbuf", "dbx")
                        v.acq("anstg")
                        inc(v.tensor_scalar(anstg[:], ain[:, 0:1024],
                                            stat[:, 7:8], None, OP.mult), "vc")
                        t.wait("vc")
                        t.acq("B6")
                        for dc in range(8):
                            t.transpose(pT[:, dc * 128:(dc + 1) * 128],
                                        anstg[:, dc * 128:(dc + 1) * 128], id_bf[:])
                        incl(t, "tn")
                        consume("anstg", "tn")
                        v.wait("tn")
                        v.acq("an_buf")
                        anv = an.ap().rearrange("p (c s u) -> p c s u", c=8, s=2)
                        inc(v.tensor_copy(anv[:, :, j, :],
                                          pT.ap().rearrange("p (c u) -> p c u", c=8)), "vc")
                        consume("B6", "vc")
                    an_ready_map[k] = semn["vc"]
                    consume("xt%d" % (k % 2), "vc")
                    consume("xt%d" % (k % 2), "sc")

                emit_xload(0)
                emit_ln(0)
                for k in range(NTILES):
                    own = k >= 8
                    trow = (k % 8) * TT
                    xbuf = xt[k % 2]
                    an_ready = an_ready_map[k]

                    if k + 1 < NTILES:
                        emit_xload(k + 1)
                    if k < 8:
                        emit_gamma(8)
                    else:
                        if not thr_emitted:
                            emit_thr()
                            thr_emitted = True
                        emit_pass2(8)

                    if k == DBGTILE and d_an2 is not None:
                        sy.wait("vc")
                        inc(sy.dma_start(d_an2, an[:]), "dbx", 16)
                        consume("an_buf", "dbx")
                    if k == DBGTILE and d_ain is not None:
                        sy.wait("vc")
                        inc(sy.dma_start(d_ain, ain[:]), "dbx", 16)
                        consume("ain", "dbx")
                    if not own:
                        v.wait("vc")
                        for dc in range(8):
                            v.reduce_sum(stat[:, 8 + dc:9 + dc], an[:, dc * TT:(dc + 1) * TT],
                                         axis=AX)
                        for dc in range(8):
                            ins = v.tensor_tensor(xsums[:, dc:dc + 1], xsums[:, dc:dc + 1],
                                                  stat[:, 8 + dc:9 + dc], OP.add)
                        inc(ins, "vc")

                    t.wait("vc", wpj_ready)
                    t.wait("vc", an_ready)
                    zmcs = list(range(16)) if (own and "zsilu" not in SKIP) else []
                    sz_acq = [False]

                    def emit_zmc(n):
                        for _ in range(n):
                            if not zmcs:
                                return
                            mc = zmcs.pop(0)
                            ps, off, rg = mslots[2 + mc % 2]
                            t.acq(rg)
                            for dc in range(8):
                                t.matmul(ps[0:128, off:off + TT],
                                         wpj[:, dc * DIP + mc * 128: dc * DIP + mc * 128 + 128],
                                         an[:, dc * TT:(dc + 1) * TT],
                                         start=(dc == 0), stop=(dc == 7))
                            incl(t, "tn")
                            s.wait("tn")
                            if not sz_acq[0]:
                                s.acq("sz")
                                sz_acq[0] = True
                            inc(s.activation(sz[:, mc * TT:(mc + 1) * TT],
                                             ps[0:128, off:off + TT], AF.Silu), "sc")
                            consume(rg, "sc")

                    if pend_k[0] != k:
                        pend[:] = list(range(16, 35))
                        pend_k[0] = k
                        pend_an[0] = an_ready
                    feed(99, allslots=True)
                    xbp_scw = semn["sc"]
                    xbp_ = xbps[k % 2]
                    dtraw_ = dtraws[k % 2]

                    consume("an_buf", "tn")
                    emit_zmc(16)
                    consume("an_buf", "tn")
                    if own:
                        tcol0 = (k - 8) * TT
                        sy.wait("tn")
                        sy.wait("bg")
                        inc(sy.dma_start(
                            s_an.rearrange("p (c u) -> p c u", c=8)[:, :, tcol0:tcol0 + TT],
                            an.ap().rearrange("p (c u) -> p c u", c=8)), "bg", 16)
                        consume("an_buf", "bg")
                    if k + 1 < NTILES:
                        emit_ln(k + 1)
                        pend[:] = list(range(16, 35))
                        pend_k[0] = k + 1
                        pend_an[0] = an_ready_map[k + 1]

                    hv = halob.ap().rearrange("p (c r) -> p c r", c=CC)
                    xbv = xbp_.ap().rearrange("p (c r) -> p c r", c=CC)
                    if k == 0:
                        inc(v.memset(xbv[:, :, 0:3], 0.0), "vc")
                    else:
                        inc(v.tensor_copy(xbv[:, :, 0:3], hv[:, :, :]), "vc")

                    g.wait("vc")
                    g.wait("tn")
                    s.acq("xcv")
                    v.wait("sc", xbp_scw)
                    g.wait("sc", xbp_scw)
                    for p0 in range(0, 14, 2):
                        accs = (cacc[0], cacc[1])
                        v.wait("sc")
                        for tap in range(4):
                            for half in range(2):
                                cc = p0 + half
                                base = cc * (TT + 3)
                                if tap == 0:
                                    ins = v.tensor_scalar(
                                        accs[half][:], xbp_[:, base:base + TT],
                                        cwt_f[:, cc * 4: cc * 4 + 1], cb_f[:, cc:cc + 1],
                                        OP.mult, OP.add)
                                else:
                                    ins = v.scalar_tensor_tensor(
                                        accs[half][:], xbp_[:, base + tap:base + tap + TT],
                                        cwt_f[:, cc * 4 + tap:cc * 4 + tap + 1],
                                        accs[half][:], OP.mult, OP.add)
                        inc(ins, "vc")
                        feed(2)
                        s.wait("vc")
                        s.activation(xcv[:, p0 * TT:(p0 + 1) * TT], accs[0][:], AF.Silu)
                        inc(s.activation(xcv[:, (p0 + 1) * TT:(p0 + 2) * TT], accs[1][:],
                                         AF.Silu), "sc")
                    for p0 in range(14, 18, 2):
                        accs = (cacc[2], cacc[3])
                        tmps = (cacc[4], cacc[5])
                        g.wait("sc")
                        for half in range(2):
                            cc = p0 + half
                            base = cc * (TT + 3)
                            g.tensor_tensor(accs[half][:], xbp_[:, base:base + TT],
                                            cwt_b[:, cc * 4:cc * 4 + 1].broadcast_to([128, TT]),
                                            OP.mult)
                        for half in range(2):
                            cc = p0 + half
                            g.tensor_tensor(accs[half][:], accs[half][:],
                                            cb_b[:, cc:cc + 1].broadcast_to([128, TT]), OP.add)
                        for tap in range(1, 4):
                            for half in range(2):
                                cc = p0 + half
                                base = cc * (TT + 3)
                                g.tensor_tensor(tmps[half][:],
                                                xbp_[:, base + tap:base + tap + TT],
                                                cwt_b[:, cc * 4 + tap:cc * 4 + tap + 1]
                                                .broadcast_to([128, TT]), OP.mult)
                            for half in range(2):
                                ins = g.tensor_tensor(accs[half][:], accs[half][:],
                                                      tmps[half][:], OP.add)
                        inc(ins, "gp")
                        s.wait("gp")
                        s.activation(xcv[:, p0 * TT:(p0 + 1) * TT], accs[0][:], AF.Silu)
                        inc(s.activation(xcv[:, (p0 + 1) * TT:(p0 + 2) * TT], accs[1][:],
                                         AF.Silu), "sc")
                    conv_done = semn["sc"]

                    v.wait("sc", conv_done)
                    ins = v.tensor_copy(hv[:, :, :], xbv[:, :, TT:TT + 3])
                    if k == 7:
                        fence(v, "vc")
                        v.tensor_scalar(halob[:], halob[:], gate_f[:, 0:1], None, OP.mult)
                        v.tensor_scalar(xsums[:], xsums[:], gate_f[:, 0:1], None, OP.mult)
                        v.tensor_scalar(Sf[:], Sf[:], gate_f[:, 0:1], None, OP.mult)
                        fence(v, "vc")
                        ins = v.tensor_copy(Sbb[:], Sf[:])
                    inc(ins, "vc")
                    consume("xbp%d" % (k % 2), "vc")
                    consume("xbp%d" % (k % 2), "gp")

                    s.wait("vc")
                    s.acq("dtsp_r")
                    s.activation(dtt1[:], dtraw_[:], AF.Abs, bias=dtb_f[:, 0:1])
                    fence(s, "sc")
                    s.activation(dtt1[:], dtt1[:], AF.Exp, scale=-1.0)
                    fence(s, "sc")
                    s.activation(dtt1[:], dtt1[:], AF.Ln, bias=1.0)
                    inc(s.activation(dtt2[:], dtraw_[:], AF.Relu, bias=dtb_f[:, 0:1]), "sc")
                    consume("dtraw%d" % (k % 2), "sc")
                    v.wait("sc")
                    v.acq("dtsp_r")
                    v.tensor_tensor(dtsp[:], dtt1[:], dtt2[:], OP.add)
                    fence(v, "vc")
                    inc(v.tensor_scalar(ldab[:], dtsp[:], nega_f[:, 0:1], None, OP.mult), "vc")
                    if own and k == 8 and d_dt is not None:
                        sy.wait("vc")
                        inc(sy.dma_start(d_dt, dtsp[:]), "dbx", 16)
                        consume("dtsp", "dbx")
                    for qi in range(CPT):
                        t.wait("vc")
                        t.acq("B7")
                        t.acq("dtT")
                        t.acq("ldatm")
                        t.matmul(pS[:, 96:128], dtsp[:, qi * 128:(qi + 1) * 128],
                                 id_f[0:32, 0:32], start=True, stop=True)
                        t.matmul(pS[:, 128:160], ldab[:, qi * 128:(qi + 1) * 128],
                                 id_bf[0:32, 0:32], start=True, stop=True)
                        incl(t, "tn")
                        consume("dtsp_r", "tn")
                        v.wait("tn")
                        v.acq("dtT")
                        v.acq("ldatm")
                        v.tensor_copy(dtT[:, qi * 32:(qi + 1) * 32], pS[:, 96:128])
                        v.tensor_copy(dtTb[:, qi * 32:(qi + 1) * 32], pS[:, 96:128])
                        inc(v.tensor_copy(ldatm[:, qi * 32:(qi + 1) * 32], pS[:, 128:160]), "vc")
                        consume("B7", "vc")

                    ldatm_ready = semn["vc"]
                    y_done = semn["tn"]
                    for qi in range(CPT):
                        toff = qi * Q
                        t.acq("B7")
                        t.matmul(pS[:, 0:32], ut_bf[:], ldatm[:, qi * 32:(qi + 1) * 32],
                                 start=True, stop=True)
                        incl(t, "tn")
                        v.wait("tn")
                        inc(v.tensor_scalar(negca[:], pS[:, 0:32], -1.0, None, OP.mult), "vc")
                        consume("B7", "vc")
                        t.wait("vc")
                        t.acq("B7")
                        t.matmul(pS[0:1, 32:64], sel_f[:], negca[:], start=True, stop=True)
                        incl(t, "tn")
                        v.wait("tn")
                        inc(v.tensor_copy(endrow[:], pS[0:1, 32:64]), "vc")
                        t.wait("vc")
                        t.matmul(pS[:, 64:96], onesr_f[0:1, 0:128], endrow[:],
                                 start=True, stop=True)
                        incl(t, "tn")
                        v.wait("tn")
                        inc(v.tensor_sub(esc[:], negca[:], pS[:, 64:96]), "vc")
                        s.wait("vc")
                        s.activation(esc[:], esc[:], AF.Exp)
                        fence(s, "sc")
                        inc(s.activation(decbc[:], pS[:, 64:96], AF.Exp, scale=-1.0), "sc")
                        consume("B7", "sc")
                        v.wait("sc")
                        inc(v.tensor_tensor(escdt[:], esc[:], dtT[:, qi * 32:(qi + 1) * 32],
                                            OP.mult), "vc")

                        t.wait("sc", conv_done)
                        t.acq("B6")
                        t.transpose(pT[:, 0:128], xcv[:, 16 * TT + toff:16 * TT + toff + 128],
                                    id_bf[:])
                        incl(t, "tn")
                        v.wait("tn")
                        inc(v.tensor_copy(btm[:], pT[:, 0:128]), "vc")
                        consume("B6", "vc")
                        for half in range(2):
                            t.acq("B6")
                            for ci in range(8):
                                cc2 = half * 8 + ci
                                t.transpose(pT[:, ci * 128:(ci + 1) * 128],
                                            xcv[:, cc2 * TT + toff: cc2 * TT + toff + 128],
                                            id_bf[:])
                            incl(t, "tn")
                            v.wait("tn")
                            inc(v.tensor_copy(xtm[:, half * 1024:(half + 1) * 1024], pT[:]), "vc")
                            consume("B6", "vc")
                        fence(v, "vc")
                        v.acq("xti")
                        xv = xtm.ap().rearrange("p (h u) -> p h u", h=H)
                        inc(v.tensor_tensor(xti.ap().rearrange("p (h u) -> p h u", h=H), xv,
                                            dtTb[:, qi * 32:(qi + 1) * 32, None]
                                            .broadcast_to([128, 32, 64]), OP.mult), "vc")
                        g.wait("vc")
                        g.acq("xtp")
                        inc(g.tensor_tensor(xtp.ap().rearrange("p (h u) -> p h u", h=H), xv,
                                            escdt[:, :, None].broadcast_to([128, 32, 64]),
                                            OP.mult), "gp")
                        xops_done = semn["vc"]
                        xtp_gp = semn["gp"]

                        if own and "groups" not in SKIP:
                            t.acq("B4")
                            t.matmul(pEF[:, 0:128], xcv[:, 16 * TT + toff:16 * TT + toff + 128],
                                     xcv[:, 17 * TT + toff:17 * TT + toff + 128],
                                     start=True, stop=True)
                            incl(t, "tn")
                            v.wait("tn")
                            inc(v.tensor_copy(gbb[:], pEF[:, 0:128]), "vc")
                            consume("B4", "vc")

                            for grp in range(4):
                                gps = pAB
                                grgs = ("B0", "B1")
                                t.acq(grgs[0])
                                t.acq(grgs[1])
                                t.wait("vc", ldatm_ready)
                                t.acq("gmt")
                                t.acq("cdec")
                                for jj in range(8):
                                    h_ = grp * 8 + jj
                                    t.matmul(gps[:, jj * 128:(jj + 1) * 128],
                                             ldatm[:, qi * 32 + h_: qi * 32 + h_ + 1]
                                             .broadcast_to([128, 128]),
                                             ut_bf[:], start=True, stop=True)
                                incl(t, "tn")
                                v.wait("tn")
                                v.wait("sc")
                                slv = sl_f.ap()[:, None, :].broadcast_to([128, 8, 128])
                                inc(v.tensor_tensor(mpre.ap().rearrange("p (a u) -> p a u", a=8),
                                                    gps.ap().rearrange("p (a u) -> p a u", a=8),
                                                    slv, OP.add), "vc")
                                if GLEVEL >= 2:
                                    s.wait("tn")
                                    s.wait("vc")
                                    s.activation(eca[:, 0:512], gps[:, 0:512], AF.Exp)
                                    inc(s.activation(eca[:, 512:1024], gps[:, 512:1024],
                                                     AF.Exp), "sc")
                                consume(grgs[0], "sc")
                                consume(grgs[1], "sc")
                                consume(grgs[0], "vc")
                                consume(grgs[1], "vc")
                                if GLEVEL >= 3:
                                    s.wait("vc")
                                    s.acq("mtb")
                                    for jj in range(8):
                                        h_ = grp * 8 + jj
                                        s.activation(mtb[:, jj * 128:(jj + 1) * 128],
                                                     mpre[:, jj * 128:(jj + 1) * 128], AF.Exp,
                                                     bias=negca[:, h_:h_ + 1])
                                    incl(s, "sc")
                                    consume("negca", "sc")
                                if GLEVEL >= 4:
                                    v.wait("sc")
                                    v.acq("gmt")
                                    v.acq("cdec")
                                    gbv = gbb.ap()[:, None, :].broadcast_to([128, 8, 128])
                                    inc(v.tensor_tensor(gmt.ap().rearrange("p (a u) -> p a u", a=8),
                                                        mtb.ap().rearrange("p (a u) -> p a u", a=8),
                                                        gbv, OP.mult), "vc")
                                    consume("mtb", "vc")
                                    cv = xcv.ap()[:, None, 17 * TT + toff:17 * TT + toff + 128] \
                                        .broadcast_to([128, 8, 128])
                                    inc(v.tensor_tensor(cdec.ap().rearrange("p (a u) -> p a u", a=8),
                                                        cv,
                                                        eca.ap().rearrange("p (a u) -> p a u", a=8),
                                                        OP.mult), "vc")
                                gmt_ready = semn["vc"]
                                for jj in range(8 if "yonly" not in SKIP else 0):
                                    h_ = grp * 8 + jj
                                    odd = h_ % 2
                                    if odd:
                                        yslot = pEF[64:128, 640:768]
                                        rgn = "B5"
                                        tp = (0, 64)
                                    else:
                                        yslot = pEF[0:64, 256:384]
                                        rgn = "B4"
                                        tp = None
                                    t.acq(rgn)
                                    t.wait("vc", gmt_ready)
                                    t.wait("sc", sbb_sc[0])
                                    t.matmul(yslot, xti[:, h_ * 64:(h_ + 1) * 64],
                                             gmt[:, jj * 128:(jj + 1) * 128],
                                             start=True, stop=False, tile_position=tp)
                                    t.matmul(yslot, Sbb[:, h_ * 64:(h_ + 1) * 64],
                                             cdec[:, jj * 128:(jj + 1) * 128],
                                             start=False, stop=True, tile_position=tp)
                                    incl(t, "tn")
                                    consume("gmt", "tn")
                                    consume("cdec", "tn")
                                    v.wait("tn")
                                    rows = slice(64, 128) if odd else slice(0, 64)
                                    ydst = yb[rows, (h_ // 2) * TT + toff:
                                              (h_ // 2) * TT + toff + 128]
                                    xsrc = xcv[rows, (h_ // 2) * TT + toff:
                                               (h_ // 2) * TT + toff + 128]
                                    inc(v.scalar_tensor_tensor(ydst, xsrc,
                                                               dbc_f[rows, h_:h_ + 1],
                                                               yslot, OP.mult, OP.add), "vc")
                                    consume(rgn, "vc")
                            y_done = semn["tn"]
                            consume("ldatm", "tn")
                            consume("xti", "tn")

                        for halfh in range(2):
                            t.acq("B2")
                            t.acq("B3")
                            t.wait("vc", xops_done)
                            t.wait("gp", xtp_gp)
                            t.wait("tn", y_done)
                            t.matmul(pCD[:, 0:512], btm[:],
                                     xtp[:, halfh * 1024: halfh * 1024 + 512],
                                     start=True, stop=True)
                            t.matmul(pCD[:, 512:1024], btm[:],
                                     xtp[:, halfh * 1024 + 512: (halfh + 1) * 1024],
                                     start=True, stop=True)
                            incl(t, "tn")
                            consume("xtp", "tn")
                            consume("btm_r", "tn")
                            v.wait("tn")
                            h0 = halfh * 16
                            dv = decbc.ap()[:, h0:h0 + 16, None].broadcast_to([128, 16, 64])
                            v.tensor_tensor(
                                mpre.ap().rearrange("p (a u) -> p a u", a=16),
                                Sf.ap().rearrange("p (a u) -> p a u", a=32)[:, h0:h0 + 16],
                                dv, OP.mult)
                            fence(v, "vc")
                            inc(v.tensor_tensor(Sf[:, h0 * 64:(h0 + 16) * 64],
                                                mpre.ap().bitcast(F32) if False else mpre[:],
                                                pCD[:], OP.add), "vc")
                            consume("B2", "vc")
                            consume("B3", "vc")
                            feed(1)
                        s.wait("vc")
                        inc(s.copy(Sbb[:], Sf[:]), "sc")
                        sbb_sc[0] = semn["sc"]

                    if own and "rs" in SKIP:
                        pass
                    elif own:
                        v.wait("sc")
                        inc(v.tensor_tensor(sz[:], yb[:], sz[:], OP.mult), "vc")
                        v.wait("vc")
                        if k == 8 and d_y is not None:
                            sy.wait("vc")
                            inc(sy.dma_start(d_y, yb[:]), "dbx", 16)
                            consume("yb", "dbx")
                        for kc in range(16):
                            sq = cacc[4 + kc % 2][:]
                            v.acq("sq%d" % (kc % 2))
                            inc(v.tensor_tensor(sq, sz[:, kc * TT:(kc + 1) * TT],
                                                sz[:, kc * TT:(kc + 1) * TT], OP.mult), "vc")
                            t.wait("vc")
                            if kc == 0:
                                t.acq("B7")
                            t.matmul(pS[0:1, 160:160 + TT], onesc_bf[:], sq,
                                     start=(kc == 0), stop=(kc == 15))
                            incl(t, "tn")
                            consume("sq%d" % (kc % 2), "tn")
                        v.wait("tn")
                        inc(v.tensor_scalar(rrow[:], pS[0:1, 160:160 + TT], 1.0 / DI, EPS,
                                            OP.mult, OP.add), "vc")
                        consume("B7", "vc")
                        s.wait("vc")
                        s.acq("rrowb")
                        s.activation(rrow[:], rrow[:], AF.Ln)
                        fence(s, "sc")
                        inc(s.activation(rrowb[:], rrow[:], AF.Exp, scale=-0.5), "sc")
                        t.wait("sc")
                        t.acq("B0")
                        t.matmul(pAB[:, 0:TT], onesr_bf[0:1, 0:128], rrowb[:],
                                 start=True, stop=True)
                        incl(t, "tn")
                        consume("rrowb", "tn")
                        v.wait("tn")
                        inc(v.tensor_copy(rsb[:], pAB[:, 0:TT]), "vc")
                        consume("B0", "vc")
                        v.wait("vc")
                        inc(v.tensor_tensor(sz.ap().rearrange("p (c u) -> p c u", c=16),
                                            sz.ap().rearrange("p (c u) -> p c u", c=16),
                                            rsb.ap()[:, None, :].broadcast_to([128, 16, TT]),
                                            OP.mult), "vc")
                        feed(2)
                        tcol = (k - 8) * TT
                        sy.wait("vc")
                        sy.wait("bg")
                        inc(sy.dma_start(
                            s_g.rearrange("p (c u) -> p c u", c=16)[:, :, tcol:tcol + TT],
                            sz.ap().rearrange("p (c u) -> p c u", c=16)), "bg", 16)
                        consume("sz", "bg")
                        if k == 8:
                            if d_zx is not None:
                                sy.wait("bg")
                                inc(sy.dma_start(d_zx, sz[:]), "dbx", 16)
                                consume("sz", "dbx")
                            if d_conv is not None:
                                inc(sy.dma_start(d_conv, xcv[:]), "dbx", 16)
                                consume("xcv", "dbx")
                            if d_xbc is not None:
                                sy.wait("dbx")
                                inc(sy.dma_start(
                                    d_xbc.rearrange("p (c u) -> p c u", c=CC),
                                    xbp_.ap().rearrange("p (c u) -> p c u",
                                                        c=CC)[:, :, 3:3 + TT]), "dbx", 16)
                                consume("xbp%d" % (k % 2), "dbx")
                    else:
                        consume("an_buf", "vc")

                    consume("an_buf", "tn")
                    consume("xcv", "tn")
                    consume("xcv", "vc")
                    if own:
                        consume("sz", "vc")

                if not thr_emitted:
                    emit_thr()
                emit_pass2(99)
                a_done = dict(semn)
                if STOP == "aphase":
                    for snx in ["tn", "vc", "sc", "gp", "o0", "o1", "qs", "bg", "dbx", "th0", "th1", "fh0", "fh1", "ws0", "ws1"]:
                        sy.wait(snx)
                    return

            with ExitStack() as tctx:
                TE_ = tctx.enter_context
                wtail = TE_(nc.sbuf_tensor("wtail", [128, 32 * 1024], BF16))
                gstr = [TE_(nc.sbuf_tensor("gstr%d" % i, [128, 16 * 128], BF16))
                        for i in range(2)]
                astr = [TE_(nc.sbuf_tensor("astr%d" % i, [128, 8 * 128], BF16))
                        for i in range(2)]
                xstr = [TE_(nc.sbuf_tensor("xstr%d" % i, [128, 1024], F32)) for i in range(2)]
                u_tm = TE_(nc.sbuf_tensor("u_tm", [128, 1024], BF16))
                crow = TE_(nc.sbuf_tensor("crow", [1, 1024], BF16))
                crowf = TE_(nc.sbuf_tensor("crowf", [1, 1024], F32))
                astg = TE_(nc.sbuf_tensor("astg", [128, 1024], F32))
                hstg = [TE_(nc.sbuf_tensor("hstg%d" % i, [128, 1024], F32)) for i in range(2)]

                barrier()

                inc(g.dma_start(wtail.ap().rearrange("p (c o) -> p c o", c=32)[:, 0:16, :],
                                w_fu[0:DI].rearrange("(c p) o -> p c o", p=128)), "wp", 16)
                v.wait("wp")
                for kc in range(16):
                    ins = v.tensor_scalar(wtail[:, kc * 1024:(kc + 1) * 1024],
                                          wtail[:, kc * 1024:(kc + 1) * 1024],
                                          gnc_f[:, kc:kc + 1], None, OP.mult)
                inc(ins, "vc")
                sy.wait("ws0")
                sy.wait("ws1")
                inc(sy.dma_start(wtail[:, 16 * 1024:24 * 1024], s_wzc[:]), "tw", 16)
                sy.wait("tw")
                inc(sy.dma_start(wtail[:, 24 * 1024:32 * 1024], s_woc[:]), "tw", 16)
                wt_vc, wt_xq = semn["vc"], semn["tw"]

                inc(v.tensor_copy(xsb[:], xsums[:]), "vc")
                t.wait("vc")
                t.wait("tw", wt_xq)
                t.acq("B4")
                t.acq("B5")
                for dc in range(8):
                    woc0 = 24 * 1024 + dc * 1024
                    t.matmul(pEF[0:1, 0:512], xsb[:, dc:dc + 1], wtail[:, woc0:woc0 + 512],
                             start=(dc == 0), stop=(dc == 7))
                    t.matmul(pEF[0:1, 512:1024], xsb[:, dc:dc + 1],
                             wtail[:, woc0 + 512:woc0 + 1024], start=(dc == 0), stop=(dc == 7))
                incl(t, "tn")
                v.wait("tn")
                v.tensor_copy(crowf[:], pEF[0:1, 0:1024])
                fence(v, "vc")
                inc(v.tensor_copy(crow[:], crowf[:]), "vc")
                consume("B4", "vc")
                consume("B5", "vc")
                crow_vc = semn["vc"]

                def tail_loads(i2):
                    bi2 = i2 % 2
                    sy.acq("gstr%d" % bi2)
                    inc(sy.dma_start(gstr[bi2].ap().rearrange("p (c u) -> p c u", c=16),
                                     s_g.rearrange("p (c u) -> p c u", c=16)
                                     [:, :, i2 * 128:(i2 + 1) * 128]), "tg%d" % bi2, 16)
                    sy.acq("astr%d" % bi2)
                    inc(sy.dma_start(astr[bi2].ap().rearrange("p (c u) -> p c u", c=8),
                                     s_an.rearrange("p (c u) -> p c u", c=8)
                                     [:, :, i2 * 128:(i2 + 1) * 128]), "ta%d" % bi2, 16)
                    sy.acq("xstr%d" % bi2)
                    inc(sy.dma_start(xstr[bi2][:], xcore[i2 * 128:(i2 + 1) * 128]),
                        "tx%d" % bi2, 16)

                a_al_map = {}

                def tail_xln(i2):
                    bi2 = i2 % 2
                    v.wait("tx%d" % bi2)
                    s.wait("tx%d" % bi2)
                    ln_stats(xstr[bi2][:], False, cb=8 * bi2)
                    a_al_map[i2] = semn["vc"]

                sy.wait("bg")
                tail_loads(0)
                tail_xln(0)
                for i in range(16):
                    bi = i % 2
                    cb = 8 * bi
                    a_al = a_al_map[i]

                    t.wait("tg%d" % bi)
                    t.wait("ta%d" % bi)
                    t.wait("tx%d" % bi)
                    t.wait("vc", wt_vc)
                    t.acq("B2")
                    t.acq("B3")
                    for dc in range(8):
                        woc0 = 24 * 1024 + dc * 1024
                        t.matmul(pCD[:, 0:512], astr[bi][:, dc * 128:(dc + 1) * 128],
                                 wtail[:, woc0:woc0 + 512], start=(dc == 0), stop=(dc == 7))
                        t.matmul(pCD[:, 512:1024], astr[bi][:, dc * 128:(dc + 1) * 128],
                                 wtail[:, woc0 + 512:woc0 + 1024],
                                 start=(dc == 0), stop=(dc == 7))
                    incl(t, "tn")
                    v.wait("tn")
                    v.acq("u_tm")
                    inc(v.tensor_copy(u_tm[:], pCD[:]), "vc")
                    consume("B2", "vc")
                    consume("B3", "vc")
                    t.wait("vc")
                    t.acq("B4")
                    t.acq("B5")
                    t.matmul(pEF[0:1, 0:512], onesc_bf[:], u_tm[:, 0:512],
                             start=True, stop=True)
                    t.matmul(pEF[0:1, 512:1024], onesc_bf[:], u_tm[:, 512:1024],
                             start=True, stop=True)
                    incl(t, "tn")

                    t.acq("B0")
                    t.acq("B1")
                    t.wait("vc", crow_vc)
                    t.matmul(pAB[:, 0:512], onesr_bf[0:1, 0:128], crow[0:1, 0:512],
                             start=True, stop=False)
                    t.matmul(pAB[:, 512:1024], onesr_bf[0:1, 0:128], crow[0:1, 512:1024],
                             start=True, stop=False)
                    carry_bcast_tn = semn["tn"] + 1
                    t.matmul(pAB[:, 0:512], ut_bf[:], u_tm[:, 0:512], start=False, stop=False)
                    t.matmul(pAB[:, 512:1024], ut_bf[:], u_tm[:, 512:1024],
                             start=False, stop=False)
                    incl(t, "tn")
                    consume("u_tm", "tn")
                    for dc in range(8):
                        wzc0 = 16 * 1024 + dc * 1024
                        t.matmul(pAB[:, 0:512], astr[bi][:, dc * 128:(dc + 1) * 128],
                                 wtail[:, wzc0:wzc0 + 512], start=False, stop=False)
                        t.matmul(pAB[:, 512:1024], astr[bi][:, dc * 128:(dc + 1) * 128],
                                 wtail[:, wzc0 + 512:wzc0 + 1024], start=False, stop=False)
                    for kc in range(16):
                        t.matmul(pAB[:, 0:512], gstr[bi][:, kc * 128:(kc + 1) * 128],
                                 wtail[:, kc * 1024:kc * 1024 + 512],
                                 start=False, stop=(kc == 15))
                        t.matmul(pAB[:, 512:1024], gstr[bi][:, kc * 128:(kc + 1) * 128],
                                 wtail[:, kc * 1024 + 512:(kc + 1) * 1024],
                                 start=False, stop=(kc == 15))
                    incl(t, "tn")
                    consume("gstr%d" % bi, "tn")
                    consume("astr%d" % bi, "tn")

                    if i + 1 < 16:
                        tail_loads(i + 1)
                        tail_xln(i + 1)

                    v.wait("tn")
                    v.tensor_tensor(crowf[:], crowf[:], pEF[0:1, 0:1024], OP.add)
                    fence(v, "vc")
                    inc(v.tensor_copy(crow[:], crowf[:]), "vc")
                    consume("B4", "vc")
                    consume("B5", "vc")
                    crow_vc = semn["vc"]

                    v.wait("vc", a_al)
                    inc(v.scalar_tensor_tensor(astg[:], xstr[bi][:], stat[:, cb + 5:cb + 6],
                                               pAB[:], OP.mult, OP.add), "vc")
                    consume("B0", "vc")
                    consume("B1", "vc")
                    v.wait("vc")
                    ln_stats(astg[:], False, cb=cb)
                    v.acq("hstg%d" % bi)
                    v.scalar_tensor_tensor(hstg[bi][:], astg[:], stat[:, cb + 5:cb + 6],
                                           xstr[bi][:], OP.mult, OP.add)
                    fence(v, "vc")
                    inc(v.tensor_scalar(hstg[bi][:], hstg[bi][:], stat[:, cb + 6:cb + 7],
                                        None, OP.add), "vc")
                    consume("xstr%d" % bi, "vc")
                    consume("xstr%d" % bi, "sc")
                    sy.wait("vc")
                    inc(sy.dma_start(s_h1[i], hstg[bi][:]), "th%d" % bi, 16)
                    consume("hstg%d" % bi, "th%d" % bi)

            if STOP == "tail":
                for snx in ["tn", "vc", "sc", "gp", "o0", "o1", "qs", "bg", "dbx", "th0", "th1", "fh0", "fh1", "ws0", "ws1"]:
                    sy.wait(snx)
                return

            with ExitStack() as fctx:
                FE = fctx.enter_context
                wq1 = FE(nc.sbuf_tensor("wq1", [128, 8 * DFF], BF16))
                h1t = [FE(nc.sbuf_tensor("h1t%d" % i, [128, 2048], F32)) for i in range(2)]
                fint = FE(nc.sbuf_tensor("fint", [128, 1024], BF16))
                finfs = [FE(nc.sbuf_tensor("finf%d" % i, [128, 8 * TT], BF16))
                         for i in range(2)]
                hhb = [FE(nc.sbuf_tensor("hhb%d" % i, [128, 32 * TT], BF16)) for i in range(2)]

                barrier()
                sy.wait("qs")
                inc(sy.dma_start(wq1[:], s_wq1[:]), "fw", 16)
                wq1_ready = semn["fw"]

                fin_ready_map = {}

                def f1_stage(k2):
                    bi2 = k2 % 2
                    sy.acq("h1t%d" % bi2)
                    inc(sy.dma_start(h1t[bi2].ap().rearrange("p (s d) -> p s d", s=2),
                                     s_h1[2 * k2:2 * k2 + 2].transpose([1, 0, 2])),
                        "f%d" % bi2, 16)
                    for j in range(2):
                        v.wait("f%d" % bi2)
                        s.wait("f%d" % bi2)
                        h1sub = h1t[bi2][:, j * 1024:(j + 1) * 1024]
                        ln_stats(h1sub, False)
                        v.acq("fint")
                        inc(v.tensor_scalar(fint[:], h1sub, stat[:, 5:6], stat[:, 6:7],
                                            OP.mult, OP.add), "vc")
                        t.wait("vc")
                        t.acq("B6")
                        for dc in range(8):
                            t.transpose(pT[:, dc * 128:(dc + 1) * 128],
                                        fint[:, dc * 128:(dc + 1) * 128], id_bf[:])
                        incl(t, "tn")
                        consume("fint", "tn")
                        v.wait("tn")
                        v.acq("finf%d" % bi2)
                        fv = finfs[bi2].ap().rearrange("p (c s u) -> p c s u", c=8, s=2)
                        inc(v.tensor_copy(fv[:, :, j, :],
                                          pT.ap().rearrange("p (c u) -> p c u", c=8)), "vc")
                        consume("B6", "vc")
                    fin_ready_map[k2] = semn["vc"]
                    consume("h1t%d" % bi2, "vc")
                    consume("h1t%d" % bi2, "sc")

                sy.wait("th0")
                sy.wait("th1")
                f1_stage(0)
                for k in range(8):
                    bi = k % 2
                    t.wait("fw", wq1_ready)
                    t.wait("vc", fin_ready_map[k])
                    for mc in range(32):
                        ps, off, rg = mslots[mc % 4]
                        t.acq(rg)
                        for dc in range(8):
                            t.matmul(ps[:, off:off + TT],
                                     wq1[:, dc * DFF + mc * 128: dc * DFF + (mc + 1) * 128],
                                     finfs[bi][:, dc * TT:(dc + 1) * TT],
                                     start=(dc == 0), stop=(dc == 7))
                        incl(t, "tn")
                        s.wait("tn")
                        if mc == 0:
                            s.acq("hh%d" % bi)
                        inc(s.activation(hhb[bi][:, mc * TT:(mc + 1) * TT], ps[:, off:off + TT],
                                         AF.Relu, bias=b1c_f[:, mc:mc + 1]), "sc")
                        consume(rg, "sc")
                    consume("finf%d" % bi, "tn")
                    if k + 1 < 8:
                        f1_stage(k + 1)
                    sy.wait("sc")
                    inc(sy.dma_start(
                        s_hh.rearrange("p (c u) -> p c u", c=32)[:, :, k * TT:(k + 1) * TT],
                        hhb[bi].ap().rearrange("p (c u) -> p c u", c=32)), "fh%d" % bi, 16)
                    consume("hh%d" % bi, "fh%d" % bi)

            if STOP == "ffn1":
                for snx in ["tn", "vc", "sc", "gp", "o0", "o1", "qs", "bg", "dbx", "th0", "th1", "fh0", "fh1", "ws0", "ws1"]:
                    sy.wait(snx)
                return

            with ExitStack() as f2ctx:
                FE2 = f2ctx.enter_context
                wq2 = FE2(nc.sbuf_tensor("wq2", [128, 32 * D], BF16))
                hhs = [FE2(nc.sbuf_tensor("hhs%d" % i, [128, 32 * 128], BF16))
                       for i in range(2)]
                h1s = [FE2(nc.sbuf_tensor("h1s%d" % i, [128, 1024], F32)) for i in range(2)]
                fout = FE2(nc.sbuf_tensor("fout", [128, 1024], F32))
                h2st = [FE2(nc.sbuf_tensor("h2st%d" % i, [128, 1024], F32)) for i in range(2)]

                barrier()
                inc(sy.dma_start(wq2[:], s_wq2[:]), "fw", 16)
                wq2_ready = semn["fw"]

                def f2_loads(i2):
                    bi2 = i2 % 2
                    sy.acq("hhs%d" % bi2)
                    inc(sy.dma_start(hhs[bi2].ap().rearrange("p (c u) -> p c u", c=32),
                                     s_hh.rearrange("p (c u) -> p c u", c=32)
                                     [:, :, i2 * 128:(i2 + 1) * 128]), "g2%d" % bi2, 16)
                    sy.acq("h1s%d" % bi2)
                    inc(sy.dma_start(h1s[bi2][:], s_h1[i2]), "h2%d" % bi2, 16)

                sy.wait("fh0")
                sy.wait("fh1")
                f2_loads(0)
                for i in range(16):
                    bi = i % 2
                    ps = pAB if bi == 0 else pCD
                    rgs = ("B0", "B1") if bi == 0 else ("B2", "B3")
                    t.wait("fw", wq2_ready)
                    t.wait("g2%d" % bi)
                    t.acq(rgs[0])
                    t.acq(rgs[1])
                    for fc in range(32):
                        t.matmul(ps[:, 0:512], hhs[bi][:, fc * 128:(fc + 1) * 128],
                                 wq2[:, fc * 1024:fc * 1024 + 512],
                                 start=(fc == 0), stop=(fc == 31))
                        t.matmul(ps[:, 512:1024], hhs[bi][:, fc * 128:(fc + 1) * 128],
                                 wq2[:, fc * 1024 + 512:(fc + 1) * 1024],
                                 start=(fc == 0), stop=(fc == 31))
                    incl(t, "tn")
                    consume("hhs%d" % bi, "tn")
                    if i + 1 < 16:
                        f2_loads(i + 1)
                    v.wait("tn")
                    v.wait("h2%d" % bi)
                    inc(v.tensor_tensor(fout[:], ps[:], b2bc_f[:], OP.add), "vc")
                    consume(rgs[0], "vc")
                    consume(rgs[1], "vc")
                    v.wait("vc")
                    ln_stats(fout[:], False)
                    v.acq("h2st%d" % bi)
                    v.scalar_tensor_tensor(h2st[bi][:], fout[:], stat[:, 5:6], h1s[bi][:],
                                           OP.mult, OP.add)
                    fence(v, "vc")
                    inc(v.tensor_scalar(h2st[bi][:], h2st[bi][:], stat[:, 6:7], None, OP.add),
                        "vc")
                    consume("h1s%d" % bi, "vc")
                    sy.wait("vc")
                    inc(sy.dma_start(out[i * 128:(i + 1) * 128], h2st[bi][:]), "o%d" % bi, 16)
                    consume("h2st%d" % bi, "o%d" % bi)

            for snx in ["tn", "vc", "sc", "gp", "o0", "o1", "qs", "bg", "dbx", "th0", "th1", "fh0", "fh1", "ws0", "ws1"]:
                sy.wait(snx)

    return nc


LAST_RES = None
_NC_CACHE = {}


def _get_nc():
    if "nc" not in _NC_CACHE:
        _NC_CACHE["nc"] = build_nc()
    return _NC_CACHE["nc"]


def kernel(**inputs):
    x = np.ascontiguousarray(np.asarray(inputs["L0_attn_x"], dtype=np.float32))
    normw = np.asarray(inputs["L0_attn_norm_w"], dtype=np.float32)
    pj = np.ascontiguousarray(np.asarray(inputs["L0_attn_in_proj_W"], dtype=np.float32))
    cw = np.asarray(inputs["L0_attn_conv_W"], dtype=np.float32)
    cb = np.asarray(inputs["L0_attn_conv_b"], dtype=np.float32)
    dtb = np.asarray(inputs["L0_attn_dt_bias"], dtype=np.float32)
    alog = np.asarray(inputs["L0_attn_A_log"], dtype=np.float32)
    Dv = np.asarray(inputs["L0_attn_D"], dtype=np.float32)
    gnw = np.asarray(inputs["L0_attn_gnorm_w"], dtype=np.float32)
    zw = np.asarray(inputs["L0_attn_zero_W"], dtype=np.float32)
    ow = np.asarray(inputs["L0_attn_one_W"], dtype=np.float32)
    fw = np.ascontiguousarray(np.asarray(inputs["L0_attn_fusion_W"], dtype=np.float32))
    w1 = np.asarray(inputs["L0_ffn_W1"], dtype=np.float32)
    b1 = np.asarray(inputs["L0_ffn_b1"], dtype=np.float32)
    w2 = np.asarray(inputs["L0_ffn_W2"], dtype=np.float32)
    b2 = np.asarray(inputs["L0_ffn_b2"], dtype=np.float32)

    consts = dict(
        w_pj=pj,
        w_zt=np.ascontiguousarray(zw.T),
        w_ot=np.ascontiguousarray(ow.T),
        w_fu=fw,
        w_q1t=np.ascontiguousarray(w1.T),
        w_q2t=np.ascontiguousarray(w2.T),
        c_ut=(np.arange(128)[:, None] <= np.arange(128)[None, :]).astype(np.float32),
        c_sl=np.where(np.arange(128)[:, None] > np.arange(128)[None, :],
                      np.float32(NEG), np.float32(0.0)).astype(np.float32),
        c_id=np.eye(128, dtype=np.float32),
        c_onesc=np.ones((128, 1), np.float32),
        c_onesr=np.ones((1, 512), np.float32),
        c_sel=(np.arange(128)[:, None] == 127).astype(np.float32),
        c_cwt=np.ascontiguousarray(cw.T),
        c_cb=np.ascontiguousarray(cb.reshape(CC, 128).T),
        c_dtb=np.ascontiguousarray(dtb.reshape(32, 1)),
        c_alog=np.ascontiguousarray(alog.reshape(32, 1)),
        c_dbc=np.ascontiguousarray(np.tile(Dv[None, :], (128, 1))),
        c_b1c=np.ascontiguousarray(b1.reshape(32, 128).T),
        c_b2bc=np.ascontiguousarray(np.tile(b2[None, :], (128, 1))),
        c_gnc=np.ascontiguousarray(gnw.reshape(16, 128).T),
        c_nwc=np.ascontiguousarray(normw.reshape(8, 128).T),
    )

    in_maps = []
    zeros = np.zeros((TPC, D), np.float32)
    for c in range(8):
        b, half = c // 2, c % 2
        m = dict(consts)
        m["xcore"] = np.ascontiguousarray(x[b, half * TPC:(half + 1) * TPC])
        m["xprev"] = np.ascontiguousarray(x[b, 0:TPC]) if half else zeros
        m["gate"] = np.full((128, 1), np.float32(half))
        in_maps.append(m)

    nc = _get_nc()
    res = run_bass_kernel_spmd(nc, in_maps[:NCORES], core_ids=list(range(NCORES)))
    o = np.zeros((B, L, D), np.float32)
    for c in range(NCORES):
        b, half = c // 2, c % 2
        o[b, half * TPC:(half + 1) * TPC] = np.asarray(res.results[c]["out"])
    if DBG:
        global LAST_RES
        LAST_RES = res
    return o


# version 21 (final)
# speedup vs baseline: 1.0103x; 1.0103x over previous
import sys
sys.path.insert(0, "/opt/trn_rl_repo")
from contextlib import ExitStack

import numpy as np
import ml_dtypes

import concourse.bass as bass
import concourse.mybir as mybir
from concourse.bass_utils import run_bass_kernel_spmd
from concourse.alu_op_type import AluOpType as OP

F32 = mybir.dt.float32
BF16 = mybir.dt.bfloat16
AF = mybir.ActivationFunctionType
AX = mybir.AxisListType.X

B, L, D = 4, 4096, 1024
TPC = 2048
TT = 256
Q = 128
CPT = TT // Q
H, HD, DS, DI = 32, 64, 128, 2048
CC = 18
DIP = 4384
DFF = 4096
NEG = -1.0e30
EPS = 1e-5

DBG = ()
DBGTILE = 8
GLEVEL = 9
SKIP = set()
STOP = "full"
NTILES = 16
NCORES = 8


RACECHECK = False


def build_nc():
    nc = bass.Bass("TRN2", target_bir_lowering=False,
                   detect_race_conditions=RACECHECK)
    dbgset = set(DBG)

    def din(name, shape):
        return nc.dram_tensor(name, list(shape), F32, kind="ExternalInput").ap()

    xcore = din("xcore", (TPC, D))
    xprev = din("xprev", (TPC, D))
    gate = din("gate", (128, 1))
    w_pj = din("w_pj", (D, DIP))
    w_zt = din("w_zt", (DI, D))
    w_ot = din("w_ot", (DI, D))
    w_fu = din("w_fu", (3 * DI, D))
    w_q1t = din("w_q1t", (D, DFF))
    w_q2t = din("w_q2t", (DFF, D))
    c_ut = din("c_ut", (128, 128))
    c_sl = din("c_sl", (128, 128))
    c_id = din("c_id", (128, 128))
    c_onesc = din("c_onesc", (128, 1))
    c_onesr = din("c_onesr", (1, 512))
    c_sel = din("c_sel", (128, 1))
    c_cwt = din("c_cwt", (CC * 128, 4))
    c_cb = din("c_cb", (128, CC))
    c_dtb = din("c_dtb", (32, 1))
    c_alog = din("c_alog", (32, 1))
    c_dbc = din("c_dbc", (128, H))
    c_b1c = din("c_b1c", (128, 32))
    c_b2bc = din("c_b2bc", (128, D))
    c_gnc = din("c_gnc", (128, 16))
    c_nwc = din("c_nwc", (128, 8))

    out = nc.dram_tensor("out", [TPC, D], F32, kind="ExternalOutput").ap()

    def scratch(name, shape, dt):
        kind = ("ExternalOutput" if (name in dbgset or name.replace("d_", "") in dbgset)
                else "Internal")
        return nc.dram_tensor(name, list(shape), dt, kind=kind).ap()

    s_wzc = scratch("s_wzc", (128, 8 * D), BF16)
    s_woc = scratch("s_woc", (128, 8 * D), BF16)
    s_g = scratch("s_g", (128, 16 * TPC), BF16)
    s_an = scratch("s_an", (128, 8 * TPC), BF16)
    s_h1 = scratch("s_h1", (16, 128, D), F32)
    s_hh = scratch("s_hh", (128, 32 * TPC), BF16)
    s_wq1 = scratch("s_wq1", (128, 8 * DFF), BF16)
    s_wq2 = scratch("s_wq2", (128, 32 * D), BF16)
    d_zx = scratch("d_zx", (128, 16 * TT), BF16) if "zx" in dbgset else None
    d_xbc = scratch("d_xbc", (128, CC * TT), BF16) if "xbc" in dbgset else None
    d_conv = scratch("d_conv", (128, CC * TT), BF16) if "conv" in dbgset else None
    d_dt = scratch("d_dt", (32, TT), F32) if "dt" in dbgset else None
    d_an2 = scratch("d_an2", (128, 8 * TT), BF16) if "an2" in dbgset else None
    d_ain = scratch("d_ain", (128, 1024), F32) if "ain" in dbgset else None
    d_stat = scratch("d_stat", (128, 16), F32) if "stat" in dbgset else None
    d_y = scratch("d_y", (128, 16 * TT), BF16) if "y" in dbgset else None

    ctx = ExitStack()
    with ctx:
        E = ctx.enter_context

        semh, semn = {}, {}
        for nm in ["tn", "vc", "sc", "gp",
                   "cg", "cy", "wz", "wf", "ws0", "ws1", "ql", "qs", "wp",
                   "x0", "x1", "bg", "dbx", "tw",
                   "tg0", "tg1", "ta0", "ta1", "tx0", "tx1", "th0", "th1",
                   "fw", "f0", "f1", "fh0", "fh1",
                   "g20", "g21", "h20", "h21", "o0", "o1"]:
            semh[nm] = E(nc.semaphore(nm))
            semn[nm] = 0
        waited = {}
        guards = {}

        class Eng:
            def __init__(self, raw, name):
                self.r = raw
                self.name = name

            def wait(self, snm, val=None):
                v = semn[snm] if val is None else val
                if v <= 0:
                    return
                key = (self.name, snm)
                if waited.get(key, 0) >= v:
                    return
                self.r.wait_ge(semh[snm], v)
                waited[key] = v

            def acq(self, region):
                for snm, val in guards.get(region, {}).items():
                    self.wait(snm, val)

            def __getattr__(self, a):
                fn = getattr(self.r, a)
                if callable(fn):
                    eng = self

                    def wrap(*args, **kw):
                        rr = fn(*args, **kw)
                        if hasattr(rr, "then_inc"):
                            eng.last = rr
                        return rr
                    return wrap
                return fn

        _inced = set()

        def inc(instr, snm, k=1):
            instr.then_inc(semh[snm], k)
            semn[snm] += k
            _inced.add(id(instr))
            return instr

        def incl(eng, snm, k=1):
            return inc(eng.last, snm, k)

        def fence(eng, snm):
            if id(eng.last) not in _inced:
                inc(eng.last, snm)
            eng.wait(snm)

        def consume(region, snm):
            d = guards.setdefault(region, {})
            d[snm] = semn[snm]

        ut_bf = E(nc.sbuf_tensor("ut_bf", [128, 128], BF16))
        sl_f = E(nc.sbuf_tensor("sl_f", [128, 128], F32))
        id_bf = E(nc.sbuf_tensor("id_bf", [128, 128], BF16))
        id_f = E(nc.sbuf_tensor("id_f", [128, 128], F32))
        onesc_bf = E(nc.sbuf_tensor("onesc_bf", [128, 1], BF16))
        onesc_f = E(nc.sbuf_tensor("onesc_f", [128, 1], F32))
        onesr_bf = E(nc.sbuf_tensor("onesr_bf", [1, 512], BF16))
        onesr_f = E(nc.sbuf_tensor("onesr_f", [1, 512], F32))
        sel_f = E(nc.sbuf_tensor("sel_f", [128, 1], F32))
        gate_f = E(nc.sbuf_tensor("gate_f", [128, 1], F32))
        cwt_f = E(nc.sbuf_tensor("cwt_f", [128, CC * 4], F32))
        cb_f = E(nc.sbuf_tensor("cb_f", [128, CC], F32))
        cwt_b = E(nc.sbuf_tensor("cwt_b", [128, CC * 4], BF16))
        cb_b = E(nc.sbuf_tensor("cb_b", [128, CC], BF16))
        dtb_f = E(nc.sbuf_tensor("dtb_f", [32, 1], F32))
        nega_f = E(nc.sbuf_tensor("nega_f", [32, 1], F32))
        dbc_f = E(nc.sbuf_tensor("dbc_f", [128, H], F32))
        b1c_f = E(nc.sbuf_tensor("b1c_f", [128, 32], F32))
        b2bc_f = E(nc.sbuf_tensor("b2bc_f", [128, D], F32))
        gnc_f = E(nc.sbuf_tensor("gnc_f", [128, 16], F32))
        nwc_f = E(nc.sbuf_tensor("nwc_f", [128, 8], F32))
        stat = E(nc.sbuf_tensor("stat", [128, 16], F32))
        xsums = E(nc.sbuf_tensor("xsums", [128, 8], F32))
        xsb = E(nc.sbuf_tensor("xsb", [128, 8], BF16))
        qcol = E(nc.sbuf_tensor("qcol", [128, 4], F32))
        thrc = E(nc.sbuf_tensor("thrc", [128, 4], F32))
        junkb = E(nc.sbuf_tensor("junkb", [128, 1024], BF16))
        qstg = E(nc.sbuf_tensor("qstg", [128, 1024], F32))
        qo1 = E(nc.sbuf_tensor("qo1", [128, 1024], BF16))
        qo2 = E(nc.sbuf_tensor("qo2", [128, 1024], BF16))

        pAB = E(nc.psum_tensor("pAB", [128, 1024], F32))
        pCD = E(nc.psum_tensor("pCD", [128, 1024], F32))
        pEF = E(nc.psum_tensor("pEF", [128, 1024], F32))
        pT = E(nc.psum_tensor("pT", [128, 1024], BF16))
        pS = E(nc.psum_tensor("pS", [128, 512], F32))

        block = E(nc.Block())

        @block.sync
        def _(_sy_raw):
            t = Eng(nc.tensor, "t")
            v = Eng(nc.vector, "v")
            s = Eng(nc.scalar, "s")
            g = Eng(nc.gpsimd, "g")
            sy = Eng(nc.sync, "y")

            def barrier(sems=None):
                if sems is None:
                    sems = list(semn.keys())
                for e in (t, v, s, g, sy):
                    for sn in sems:
                        e.wait(sn)

            def softsync(dma=False):
                sems = tuple(semn.keys()) if dma else ("tn", "vc", "sc", "gp")
                for e in (t, v, s, g):
                    for sn in sems:
                        e.wait(sn)

            for dst, srcap in [(ut_bf[:], c_ut), (id_bf[:], c_id),
                               (onesc_bf[:], c_onesc), (onesr_bf[:], c_onesr)]:
                g.wait("cg")
                inc(g.dma_start(dst, srcap), "cg", 16)
            for dst, srcap in [
                    (sl_f[:], c_sl), (id_f[:], c_id), (onesc_f[:], c_onesc),
                    (onesr_f[:], c_onesr), (sel_f[:], c_sel), (gate_f[:], gate),
                    (cwt_f.ap().rearrange("p (c k) -> p c k", c=CC),
                     c_cwt.rearrange("(c p) k -> p c k", p=128)),
                    (cb_f[:], c_cb), (dtb_f[:], c_dtb), (nega_f[:], c_alog),
                    (dbc_f[:], c_dbc), (b1c_f[:], c_b1c), (b2bc_f[:], c_b2bc),
                    (gnc_f[:], c_gnc), (nwc_f[:], c_nwc)]:
                sy.wait("cy")
                inc(sy.dma_start(dst, srcap), "cy", 16)
            g.wait("cg")
            inc(g.dma_start(cwt_b.ap().rearrange("p (c k) -> p c k", c=CC),
                            c_cwt.rearrange("(c p) k -> p c k", p=128)), "cg", 16)
            g.wait("cg")
            inc(g.dma_start(cb_b[:], c_cb), "cg", 16)

            s.wait("cy")
            inc(s.activation(nega_f[:], nega_f[:], AF.Exp), "sc")
            v.wait("sc")
            v.wait("cg")
            v.wait("cy")
            inc(v.tensor_scalar(nega_f[:], nega_f[:], -1.0, None, OP.mult), "vc")
            v.memset(xsums[:], 0.0)
            inc(v.memset(qcol[:], 0.0), "vc")

            if STOP == "consts":
                for snx in ["vc", "sc"]:
                    sy.wait(snx)
                return

            def qchunk(mi, c):
                if mi == 0:
                    srcr = w_q1t.rearrange("(g p) d -> p g d", p=128)
                    return srcr[:, c // 4:c // 4 + 1, (c % 4) * 1024:(c % 4 + 1) * 1024], c * 1024
                srcr = w_q2t.rearrange("(g p) d -> p g d", p=128)
                return srcr[:, c:c + 1, :], c * 1024

            gamma_chunks = [(mi, c) for mi in range(2) for c in range(32)]
            pass2_chunks = list(gamma_chunks)

            def emit_gamma(n):
                for _ in range(n):
                    if not gamma_chunks:
                        return
                    mi, c = gamma_chunks.pop(0)
                    srcq, _col = qchunk(mi, c)
                    g.wait("sc")
                    inc(g.dma_start(qstg.ap().rearrange("p (a d) -> p a d", a=srcq.shape[1]),
                                    srcq), "ql", 16)
                    s.wait("ql")
                    s.wait("vc")
                    inc(s.activation(qo1[:], qstg[:], AF.Abs, accum_out=qcol[:, 2:3]), "sc")
                    v.wait("sc")
                    inc(v.tensor_tensor(qcol[:, mi:mi + 1], qcol[:, mi:mi + 1],
                                        qcol[:, 2:3], OP.add), "vc")

            def emit_thr():
                t.wait("vc")
                t.acq("B7")
                for mi in range(2):
                    t.matmul(pS[0:1, mi:mi + 1], qcol[:, mi:mi + 1], onesc_f[:],
                             start=True, stop=True)
                incl(t, "tn")
                v.wait("tn")
                NW = float(DFF * D)
                inc(v.tensor_scalar(stat[0:1, 8:10], pS[0:1, 0:2], 0.5 / NW, 0.5 * EPS,
                                    OP.mult, OP.add), "vc")
                consume("B7", "vc")
                t.wait("vc")
                t.acq("B7")
                t.matmul(pS[:, 4:6], onesr_f[0:1, 0:128], stat[0:1, 8:10], start=True, stop=True)
                incl(t, "tn")
                v.wait("tn")
                v.tensor_copy(thrc[:, 0:2], pS[:, 4:6])
                fence(v, "vc")
                inc(v.tensor_scalar(thrc[:, 2:4], thrc[:, 0:2], -1.0, None, OP.mult), "vc")
                consume("B7", "vc")

            def emit_pass2(n):
                for _ in range(n):
                    if not pass2_chunks:
                        return
                    mi, c = pass2_chunks.pop(0)
                    srcq, col = qchunk(mi, c)
                    g.wait("vc")
                    g.wait("qs")
                    inc(g.dma_start(qstg.ap().rearrange("p (a d) -> p a d", a=srcq.shape[1]),
                                    srcq), "ql", 16)
                    v.wait("ql")
                    v.wait("qs")
                    v.tensor_scalar(qo1[:], qstg[:], thrc[:, mi:mi + 1], None, OP.is_gt)
                    inc(v.tensor_scalar(qo2[:], qstg[:], thrc[:, 2 + mi:3 + mi],
                                        None, OP.is_lt), "vc")
                    g.wait("vc")
                    inc(g.tensor_tensor(qo1[:], qo1[:], qo2[:], OP.subtract), "gp")
                    dst = s_wq1 if mi == 0 else s_wq2
                    inc(g.dma_start(dst[:, col:col + 1024], qo1[:]), "qs", 16)

            with ExitStack() as sctx:
                SE = sctx.enter_context
                zt = SE(nc.sbuf_tensor("zt", [128, 16 * D], BF16))
                fz = SE(nc.sbuf_tensor("fz", [128, 16 * D], BF16))
                wstg = SE(nc.sbuf_tensor("wstg", [128, 2 * D], BF16))

                for wi in range(2):
                    wt_src = w_zt if wi == 0 else w_ot
                    r0 = DI if wi == 0 else 2 * DI
                    s_dst = s_wzc if wi == 0 else s_woc
                    if wi == 1:
                        g.wait("tn")
                    inc(g.dma_start(zt.ap().rearrange("p (c i) -> p c i", c=16),
                                    wt_src.rearrange("(c p) i -> p c i", p=128)), "wz", 16)
                    inc(g.dma_start(fz.ap().rearrange("p (c o) -> p c o", c=16),
                                    w_fu[r0:r0 + DI].rearrange("(c p) o -> p c o", p=128)),
                        "wf", 16)
                    t.wait("wz")
                    t.wait("wf")
                    t.wait("vc")
                    if STOP == "wzload":
                        sy.wait("wz")
                        sy.wait("wf")
                        return
                    for ic in range(8):
                        ps = pAB if ic % 2 == 0 else pCD
                        rgs = ("B0", "B1") if ic % 2 == 0 else ("B2", "B3")
                        t.acq(rgs[0])
                        t.acq(rgs[1])
                        for jc in range(16):
                            lt = zt[:, jc * D + ic * 128: jc * D + (ic + 1) * 128]
                            t.matmul(ps[:, 0:512], lt, fz[:, jc * D: jc * D + 512],
                                     start=(jc == 0), stop=(jc == 15))
                            t.matmul(ps[:, 512:1024], lt, fz[:, jc * D + 512: jc * D + 1024],
                                     start=(jc == 0), stop=(jc == 15))
                        incl(t, "tn")
                        v.wait("tn")
                        v.acq("wstg%d" % (ic % 2))
                        wslot = wstg[:, (ic % 2) * D:(ic % 2) * D + D]
                        inc(v.tensor_scalar(wslot, ps[:, 0:1024], nwc_f[:, ic:ic + 1],
                                            None, OP.mult), "vc")
                        consume(rgs[0], "vc")
                        consume(rgs[1], "vc")
                        sy.wait("vc")
                        inc(sy.dma_start(s_dst[:, ic * D: ic * D + D], wslot),
                            "ws%d" % (ic % 2), 16)
                        consume("wstg%d" % (ic % 2), "ws%d" % (ic % 2))
                        if STOP == "wz1":
                            sy.wait("ws0")
                            return
                startup_done = dict(semn)

            if STOP == "startup":
                for snx in ["tn", "vc", "sc", "gp", "o0", "o1", "qs", "bg", "dbx", "th0", "th1", "fh0", "fh1", "ws0", "ws1"]:
                    sy.wait(snx)
                return

            with ExitStack() as actx:
                AE = actx.enter_context
                wpj = AE(nc.sbuf_tensor("wpj", [128, 8 * DIP], BF16))
                xt = [AE(nc.sbuf_tensor("xt%d" % i, [128, 2048], F32)) for i in range(2)]
                ain = AE(nc.sbuf_tensor("ain", [128, 1024], F32))
                anstg = AE(nc.sbuf_tensor("anstg", [128, 1024], BF16))
                an = AE(nc.sbuf_tensor("an", [128, 8 * TT], BF16))
                sz = AE(nc.sbuf_tensor("sz", [128, 16 * TT], BF16))
                xbps = [AE(nc.sbuf_tensor("xbp%d" % i, [128, CC * (TT + 3)], BF16))
                        for i in range(2)]
                xcv = AE(nc.sbuf_tensor("xcv", [128, CC * TT], BF16))
                halob = AE(nc.sbuf_tensor("halob", [128, CC * 3], BF16))
                cacc = [AE(nc.sbuf_tensor("cacc%d" % i, [128, TT], BF16)) for i in range(6)]
                dtraws = [AE(nc.sbuf_tensor("dtraw%d" % i, [32, TT], F32)) for i in range(2)]
                dtt1 = AE(nc.sbuf_tensor("dtt1", [32, TT], F32))
                dtt2 = AE(nc.sbuf_tensor("dtt2", [32, TT], F32))
                dtsp = AE(nc.sbuf_tensor("dtsp", [32, TT], F32))
                ldab = AE(nc.sbuf_tensor("ldab", [32, TT], BF16))
                dtT = AE(nc.sbuf_tensor("dtT", [128, 2 * 32], F32))
                dtTb = AE(nc.sbuf_tensor("dtTb", [128, 2 * 32], BF16))
                ldatm = AE(nc.sbuf_tensor("ldatm", [128, 2 * 32], BF16))
                negca = AE(nc.sbuf_tensor("negca", [128, 32], F32))
                endrow = AE(nc.sbuf_tensor("endrow", [1, 32], F32))
                esc = AE(nc.sbuf_tensor("esc", [128, 32], F32))
                escdt = AE(nc.sbuf_tensor("escdt", [128, 32], BF16))
                decbc = AE(nc.sbuf_tensor("decbc", [128, 32], F32))
                btm = AE(nc.sbuf_tensor("btm", [128, 128], BF16))
                xtm = AE(nc.sbuf_tensor("xtm", [128, 2048], BF16))
                xti = AE(nc.sbuf_tensor("xti", [128, 2048], BF16))
                xtp = AE(nc.sbuf_tensor("xtp", [128, 2048], BF16))
                mpre = AE(nc.sbuf_tensor("mpre", [128, 1024], F32))
                mtb = AE(nc.sbuf_tensor("mtb", [128, 1024], BF16))
                eca = AE(nc.sbuf_tensor("eca", [128, 1024], BF16))
                gmt = AE(nc.sbuf_tensor("gmt", [128, 1024], BF16))
                cdec = AE(nc.sbuf_tensor("cdec", [128, 1024], BF16))
                gbb = AE(nc.sbuf_tensor("gbb", [128, 128], BF16))
                Sf = AE(nc.sbuf_tensor("Sf", [128, 2048], F32))
                Sbb = AE(nc.sbuf_tensor("Sbb", [128, 2048], BF16))
                yb = AE(nc.sbuf_tensor("yb", [128, 16 * TT], BF16))
                rrow = AE(nc.sbuf_tensor("rrow", [1, TT], F32))
                rrowb = AE(nc.sbuf_tensor("rrowb", [1, TT], BF16))
                rsb = AE(nc.sbuf_tensor("rsb", [128, TT], BF16))

                barrier()

                inc(g.dma_start(wpj.ap().rearrange("p (c m) -> p c m", c=8),
                                w_pj.rearrange("(c p) m -> p c m", p=128)), "wp", 16)
                v.wait("wp")
                for dc in range(8):
                    ins = v.tensor_scalar(wpj[:, dc * DIP:(dc + 1) * DIP],
                                          wpj[:, dc * DIP:(dc + 1) * DIP],
                                          nwc_f[:, dc:dc + 1], None, OP.mult)
                inc(ins, "vc")
                wpj_ready = semn["vc"]
                v.memset(Sf[:], 0.0)
                inc(v.memset(Sbb[:], 0.0), "vc")

                def ln_stats(xin, want_rsn, cb=0):
                    C = lambda n: stat[:, cb + n:cb + n + 1]
                    s.wait("vc")
                    s.activation(junkb[:], xin, AF.Identity, accum_out=C(0))
                    fence(s, "sc")
                    inc(s.activation(junkb[:], xin, AF.Square, accum_out=C(1)), "sc")
                    v.wait("sc")
                    v.tensor_scalar(C(2), C(0), 1.0 / D, None, OP.mult)
                    v.tensor_scalar(C(3), C(1), -1.0 / D, None, OP.mult)
                    fence(v, "vc")
                    v.scalar_tensor_tensor(C(3), C(2), C(2),
                                           C(3), OP.mult, OP.add)
                    fence(v, "vc")
                    inc(v.tensor_scalar(C(4), C(3), -1.0, EPS,
                                        OP.mult, OP.add), "vc")
                    s.wait("vc")
                    s.activation(C(5), C(4), AF.Ln)
                    fence(s, "sc")
                    inc(s.activation(C(5), C(5), AF.Exp, scale=-0.5), "sc")
                    v.wait("sc")
                    v.tensor_scalar(C(6), C(2), -1.0, None, OP.mult)
                    fence(v, "vc")
                    ins2 = v.tensor_tensor(C(6), C(6), C(5), OP.mult)
                    if not want_rsn:
                        inc(ins2, "vc")
                        v.wait("vc")
                        return
                    fence(v, "vc")
                    v.reciprocal(C(7), C(4))
                    v.tensor_scalar(C(3), C(3), -1.0, None, OP.mult)
                    fence(v, "vc")
                    v.tensor_tensor(C(7), C(7), C(3), OP.mult)
                    fence(v, "vc")
                    inc(v.tensor_scalar(C(7), C(7), 1.0, EPS,
                                        OP.mult, OP.add), "vc")
                    s.wait("vc")
                    s.activation(C(7), C(7), AF.Ln)
                    fence(s, "sc")
                    inc(s.activation(C(7), C(7), AF.Exp, scale=-0.5), "sc")
                    v.wait("sc")

                mslots = [(pAB, 0, "B0"), (pAB, 512, "B1"), (pCD, 0, "B2"), (pCD, 512, "B3")]
                thr_emitted = False
                an_ready_map = {}
                sbb_sc = [0]

                pend = []
                pend_k = [None]
                pend_an = [0]
                feed_rot = [0]

                def feed(n=1, allslots=True):
                    for _ in range(n):
                        if not pend:
                            return
                        mc = pend.pop(0)
                        kk = pend_k[0]
                        si = feed_rot[0] % 4
                        feed_rot[0] += 1
                        ps, off, rg = mslots[si]
                        t.acq(rg)
                        t.wait("vc", wpj_ready)
                        t.wait("vc", pend_an[0])
                        mw = 32 if mc == 34 else 128
                        for dc in range(8):
                            t.matmul(ps[0:mw, off:off + TT],
                                     wpj[:, dc * DIP + mc * 128: dc * DIP + mc * 128 + mw],
                                     an[:, dc * TT:(dc + 1) * TT],
                                     start=(dc == 0), stop=(dc == 7))
                        incl(t, "tn")
                        s.wait("tn")
                        if mc < 34:
                            cc = mc - 16
                            s.acq("xbp%d" % (kk % 2))
                            inc(s.copy(
                                xbps[kk % 2][:, cc * (TT + 3) + 3: cc * (TT + 3) + 3 + TT],
                                ps[0:128, off:off + TT]), "sc")
                        else:
                            s.acq("dtraw%d" % (kk % 2))
                            inc(s.copy(dtraws[kk % 2][:], ps[0:32, off:off + TT]),
                                "sc")
                        consume(rg, "sc")

                xld_map = {}

                def emit_xload(k):
                    srcx = xcore if k >= 8 else xprev
                    trow2 = (k % 8) * TT
                    xbuf2 = xt[k % 2]
                    sy.acq("xt%d" % (k % 2))
                    xsem = "x%d" % (k % 2)
                    inc(sy.dma_start(xbuf2.ap().rearrange("p (s d) -> p s d", s=2),
                                     srcx[trow2:trow2 + TT].rearrange("(s p) d -> p s d",
                                                                      p=128)), xsem, 16)
                    xld_map[k] = semn[xsem]

                def emit_ln(k):
                    xbuf2 = xt[k % 2]
                    xsem = "x%d" % (k % 2)
                    xld = xld_map[k]
                    for j in range(2):
                        v.wait(xsem, xld)
                        s.wait(xsem, xld)
                        xsub = xbuf2[:, j * 1024:(j + 1) * 1024]
                        ln_stats(xsub, True)
                        v.acq("ain_r")
                        v.tensor_scalar(ain[:, 0:1024], xsub,
                                        stat[:, 5:6], stat[:, 6:7], OP.mult, OP.add)
                        fence(v, "vc")
                        if k == DBGTILE and j == 0 and d_stat is not None:
                            sy.wait("vc")
                            inc(sy.dma_start(d_stat, stat[:]), "dbx", 16)
                            consume("statbuf", "dbx")
                        v.acq("anstg")
                        inc(v.tensor_scalar(anstg[:], ain[:, 0:1024],
                                            stat[:, 7:8], None, OP.mult), "vc")
                        t.wait("vc")
                        t.acq("B6")
                        for dc in range(8):
                            t.transpose(pT[:, dc * 128:(dc + 1) * 128],
                                        anstg[:, dc * 128:(dc + 1) * 128], id_bf[:])
                        incl(t, "tn")
                        consume("anstg", "tn")
                        v.wait("tn")
                        v.acq("an_buf")
                        anv = an.ap().rearrange("p (c s u) -> p c s u", c=8, s=2)
                        inc(v.tensor_copy(anv[:, :, j, :],
                                          pT.ap().rearrange("p (c u) -> p c u", c=8)), "vc")
                        consume("B6", "vc")
                    an_ready_map[k] = semn["vc"]
                    consume("xt%d" % (k % 2), "vc")
                    consume("xt%d" % (k % 2), "sc")

                emit_xload(0)
                emit_ln(0)
                for k in range(NTILES):
                    own = k >= 8
                    trow = (k % 8) * TT
                    xbuf = xt[k % 2]
                    an_ready = an_ready_map[k]

                    if k + 1 < NTILES:
                        emit_xload(k + 1)
                    if k < 8:
                        emit_gamma(8)
                    else:
                        if not thr_emitted:
                            emit_thr()
                            thr_emitted = True
                        emit_pass2(8)

                    if k == DBGTILE and d_an2 is not None:
                        sy.wait("vc")
                        inc(sy.dma_start(d_an2, an[:]), "dbx", 16)
                        consume("an_buf", "dbx")
                    if k == DBGTILE and d_ain is not None:
                        sy.wait("vc")
                        inc(sy.dma_start(d_ain, ain[:]), "dbx", 16)
                        consume("ain", "dbx")
                    if not own:
                        v.wait("vc")
                        for dc in range(8):
                            v.reduce_sum(stat[:, 8 + dc:9 + dc], an[:, dc * TT:(dc + 1) * TT],
                                         axis=AX)
                        for dc in range(8):
                            ins = v.tensor_tensor(xsums[:, dc:dc + 1], xsums[:, dc:dc + 1],
                                                  stat[:, 8 + dc:9 + dc], OP.add)
                        inc(ins, "vc")

                    t.wait("vc", wpj_ready)
                    t.wait("vc", an_ready)
                    zmcs = list(range(16)) if (own and "zsilu" not in SKIP) else []
                    sz_acq = [False]

                    def emit_zmc(n):
                        for _ in range(n):
                            if not zmcs:
                                return
                            mc = zmcs.pop(0)
                            ps, off, rg = mslots[2 + mc % 2]
                            t.acq(rg)
                            for dc in range(8):
                                t.matmul(ps[0:128, off:off + TT],
                                         wpj[:, dc * DIP + mc * 128: dc * DIP + mc * 128 + 128],
                                         an[:, dc * TT:(dc + 1) * TT],
                                         start=(dc == 0), stop=(dc == 7))
                            incl(t, "tn")
                            s.wait("tn")
                            if not sz_acq[0]:
                                s.acq("sz")
                                sz_acq[0] = True
                            inc(s.activation(sz[:, mc * TT:(mc + 1) * TT],
                                             ps[0:128, off:off + TT], AF.Silu), "sc")
                            consume(rg, "sc")

                    if pend_k[0] != k:
                        pend[:] = list(range(16, 35))
                        pend_k[0] = k
                        pend_an[0] = an_ready
                    feed(99, allslots=True)
                    xbp_scw = semn["sc"]
                    xbp_ = xbps[k % 2]
                    dtraw_ = dtraws[k % 2]

                    consume("an_buf", "tn")
                    emit_zmc(16)
                    consume("an_buf", "tn")
                    if own:
                        tcol0 = (k - 8) * TT
                        sy.wait("tn")
                        sy.wait("bg")
                        inc(sy.dma_start(
                            s_an.rearrange("p (c u) -> p c u", c=8)[:, :, tcol0:tcol0 + TT],
                            an.ap().rearrange("p (c u) -> p c u", c=8)), "bg", 16)
                        consume("an_buf", "bg")
                    if k + 1 < NTILES:
                        emit_ln(k + 1)
                        pend[:] = list(range(16, 35))
                        pend_k[0] = k + 1
                        pend_an[0] = an_ready_map[k + 1]

                    hv = halob.ap().rearrange("p (c r) -> p c r", c=CC)
                    xbv = xbp_.ap().rearrange("p (c r) -> p c r", c=CC)
                    if k == 0:
                        inc(v.memset(xbv[:, :, 0:3], 0.0), "vc")
                    else:
                        inc(v.tensor_copy(xbv[:, :, 0:3], hv[:, :, :]), "vc")

                    g.wait("vc")
                    g.wait("tn")
                    s.acq("xcv")
                    v.wait("sc", xbp_scw)
                    g.wait("sc", xbp_scw)
                    for p0 in range(0, 14, 2):
                        accs = (cacc[0], cacc[1])
                        v.wait("sc")
                        for tap in range(4):
                            for half in range(2):
                                cc = p0 + half
                                base = cc * (TT + 3)
                                if tap == 0:
                                    ins = v.tensor_scalar(
                                        accs[half][:], xbp_[:, base:base + TT],
                                        cwt_f[:, cc * 4: cc * 4 + 1], cb_f[:, cc:cc + 1],
                                        OP.mult, OP.add)
                                else:
                                    ins = v.scalar_tensor_tensor(
                                        accs[half][:], xbp_[:, base + tap:base + tap + TT],
                                        cwt_f[:, cc * 4 + tap:cc * 4 + tap + 1],
                                        accs[half][:], OP.mult, OP.add)
                        inc(ins, "vc")
                        feed(2)
                        s.wait("vc")
                        s.activation(xcv[:, p0 * TT:(p0 + 1) * TT], accs[0][:], AF.Silu)
                        inc(s.activation(xcv[:, (p0 + 1) * TT:(p0 + 2) * TT], accs[1][:],
                                         AF.Silu), "sc")
                    for p0 in range(14, 18, 2):
                        accs = (cacc[2], cacc[3])
                        tmps = (cacc[4], cacc[5])
                        g.wait("sc")
                        for half in range(2):
                            cc = p0 + half
                            base = cc * (TT + 3)
                            g.tensor_tensor(accs[half][:], xbp_[:, base:base + TT],
                                            cwt_b[:, cc * 4:cc * 4 + 1].broadcast_to([128, TT]),
                                            OP.mult)
                        for half in range(2):
                            cc = p0 + half
                            g.tensor_tensor(accs[half][:], accs[half][:],
                                            cb_b[:, cc:cc + 1].broadcast_to([128, TT]), OP.add)
                        for tap in range(1, 4):
                            for half in range(2):
                                cc = p0 + half
                                base = cc * (TT + 3)
                                g.tensor_tensor(tmps[half][:],
                                                xbp_[:, base + tap:base + tap + TT],
                                                cwt_b[:, cc * 4 + tap:cc * 4 + tap + 1]
                                                .broadcast_to([128, TT]), OP.mult)
                            for half in range(2):
                                ins = g.tensor_tensor(accs[half][:], accs[half][:],
                                                      tmps[half][:], OP.add)
                        inc(ins, "gp")
                        s.wait("gp")
                        s.activation(xcv[:, p0 * TT:(p0 + 1) * TT], accs[0][:], AF.Silu)
                        inc(s.activation(xcv[:, (p0 + 1) * TT:(p0 + 2) * TT], accs[1][:],
                                         AF.Silu), "sc")
                    conv_done = semn["sc"]

                    v.wait("sc", conv_done)
                    ins = v.tensor_copy(hv[:, :, :], xbv[:, :, TT:TT + 3])
                    if k == 7:
                        fence(v, "vc")
                        v.tensor_scalar(halob[:], halob[:], gate_f[:, 0:1], None, OP.mult)
                        v.tensor_scalar(xsums[:], xsums[:], gate_f[:, 0:1], None, OP.mult)
                        v.tensor_scalar(Sf[:], Sf[:], gate_f[:, 0:1], None, OP.mult)
                        fence(v, "vc")
                        ins = v.tensor_copy(Sbb[:], Sf[:])
                    inc(ins, "vc")
                    consume("xbp%d" % (k % 2), "vc")
                    consume("xbp%d" % (k % 2), "gp")

                    s.wait("vc")
                    s.acq("dtsp_r")
                    s.activation(dtt1[:], dtraw_[:], AF.Abs, bias=dtb_f[:, 0:1])
                    fence(s, "sc")
                    s.activation(dtt1[:], dtt1[:], AF.Exp, scale=-1.0)
                    fence(s, "sc")
                    s.activation(dtt1[:], dtt1[:], AF.Ln, bias=1.0)
                    inc(s.activation(dtt2[:], dtraw_[:], AF.Relu, bias=dtb_f[:, 0:1]), "sc")
                    consume("dtraw%d" % (k % 2), "sc")
                    v.wait("sc")
                    v.acq("dtsp_r")
                    v.tensor_tensor(dtsp[:], dtt1[:], dtt2[:], OP.add)
                    fence(v, "vc")
                    inc(v.tensor_scalar(ldab[:], dtsp[:], nega_f[:, 0:1], None, OP.mult), "vc")
                    if own and k == 8 and d_dt is not None:
                        sy.wait("vc")
                        inc(sy.dma_start(d_dt, dtsp[:]), "dbx", 16)
                        consume("dtsp", "dbx")
                    for qi in range(CPT):
                        t.wait("vc")
                        t.acq("B7")
                        t.acq("dtT")
                        t.acq("ldatm")
                        t.matmul(pS[:, 96:128], dtsp[:, qi * 128:(qi + 1) * 128],
                                 id_f[0:32, 0:32], start=True, stop=True)
                        t.matmul(pS[:, 128:160], ldab[:, qi * 128:(qi + 1) * 128],
                                 id_bf[0:32, 0:32], start=True, stop=True)
                        incl(t, "tn")
                        consume("dtsp_r", "tn")
                        v.wait("tn")
                        v.acq("dtT")
                        v.acq("ldatm")
                        v.tensor_copy(dtT[:, qi * 32:(qi + 1) * 32], pS[:, 96:128])
                        v.tensor_copy(dtTb[:, qi * 32:(qi + 1) * 32], pS[:, 96:128])
                        inc(v.tensor_copy(ldatm[:, qi * 32:(qi + 1) * 32], pS[:, 128:160]), "vc")
                        consume("B7", "vc")

                    ldatm_ready = semn["vc"]
                    y_done = semn["tn"]
                    for qi in range(CPT):
                        toff = qi * Q
                        t.acq("B7")
                        t.matmul(pS[:, 0:32], ut_bf[:], ldatm[:, qi * 32:(qi + 1) * 32],
                                 start=True, stop=True)
                        incl(t, "tn")
                        v.wait("tn")
                        inc(v.tensor_scalar(negca[:], pS[:, 0:32], -1.0, None, OP.mult), "vc")
                        consume("B7", "vc")
                        t.wait("vc")
                        t.acq("B7")
                        t.matmul(pS[0:1, 32:64], sel_f[:], negca[:], start=True, stop=True)
                        incl(t, "tn")
                        v.wait("tn")
                        inc(v.tensor_copy(endrow[:], pS[0:1, 32:64]), "vc")
                        t.wait("vc")
                        t.matmul(pS[:, 64:96], onesr_f[0:1, 0:128], endrow[:],
                                 start=True, stop=True)
                        incl(t, "tn")
                        v.wait("tn")
                        inc(v.tensor_sub(esc[:], negca[:], pS[:, 64:96]), "vc")
                        s.wait("vc")
                        s.activation(esc[:], esc[:], AF.Exp)
                        fence(s, "sc")
                        inc(s.activation(decbc[:], pS[:, 64:96], AF.Exp, scale=-1.0), "sc")
                        consume("B7", "sc")
                        v.wait("sc")
                        inc(v.tensor_tensor(escdt[:], esc[:], dtT[:, qi * 32:(qi + 1) * 32],
                                            OP.mult), "vc")

                        t.wait("sc", conv_done)
                        t.acq("B6")
                        t.transpose(pT[:, 0:128], xcv[:, 16 * TT + toff:16 * TT + toff + 128],
                                    id_bf[:])
                        incl(t, "tn")
                        v.wait("tn")
                        inc(v.tensor_copy(btm[:], pT[:, 0:128]), "vc")
                        consume("B6", "vc")
                        for half in range(2):
                            t.acq("B6")
                            for ci in range(8):
                                cc2 = half * 8 + ci
                                t.transpose(pT[:, ci * 128:(ci + 1) * 128],
                                            xcv[:, cc2 * TT + toff: cc2 * TT + toff + 128],
                                            id_bf[:])
                            incl(t, "tn")
                            v.wait("tn")
                            inc(v.tensor_copy(xtm[:, half * 1024:(half + 1) * 1024], pT[:]), "vc")
                            consume("B6", "vc")
                        fence(v, "vc")
                        v.acq("xti")
                        xv = xtm.ap().rearrange("p (h u) -> p h u", h=H)
                        inc(v.tensor_tensor(xti.ap().rearrange("p (h u) -> p h u", h=H), xv,
                                            dtTb[:, qi * 32:(qi + 1) * 32, None]
                                            .broadcast_to([128, 32, 64]), OP.mult), "vc")
                        g.wait("vc")
                        g.acq("xtp")
                        inc(g.tensor_tensor(xtp.ap().rearrange("p (h u) -> p h u", h=H), xv,
                                            escdt[:, :, None].broadcast_to([128, 32, 64]),
                                            OP.mult), "gp")
                        xops_done = semn["vc"]
                        xtp_gp = semn["gp"]

                        if own and "groups" not in SKIP:
                            t.acq("B4")
                            t.matmul(pEF[:, 0:128], xcv[:, 16 * TT + toff:16 * TT + toff + 128],
                                     xcv[:, 17 * TT + toff:17 * TT + toff + 128],
                                     start=True, stop=True)
                            incl(t, "tn")
                            v.wait("tn")
                            inc(v.tensor_copy(gbb[:], pEF[:, 0:128]), "vc")
                            consume("B4", "vc")

                            for grp in range(4):
                                gps = pAB
                                grgs = ("B0", "B1")
                                t.acq(grgs[0])
                                t.acq(grgs[1])
                                t.wait("vc", ldatm_ready)
                                t.acq("gmt")
                                t.acq("cdec")
                                for jj in range(8):
                                    h_ = grp * 8 + jj
                                    t.matmul(gps[:, jj * 128:(jj + 1) * 128],
                                             ldatm[:, qi * 32 + h_: qi * 32 + h_ + 1]
                                             .broadcast_to([128, 128]),
                                             ut_bf[:], start=True, stop=True)
                                incl(t, "tn")
                                v.wait("tn")
                                v.wait("sc")
                                slv = sl_f.ap()[:, None, :].broadcast_to([128, 8, 128])
                                inc(v.tensor_tensor(mpre.ap().rearrange("p (a u) -> p a u", a=8),
                                                    gps.ap().rearrange("p (a u) -> p a u", a=8),
                                                    slv, OP.add), "vc")
                                if GLEVEL >= 2:
                                    s.wait("tn")
                                    s.wait("vc")
                                    s.activation(eca[:, 0:512], gps[:, 0:512], AF.Exp)
                                    inc(s.activation(eca[:, 512:1024], gps[:, 512:1024],
                                                     AF.Exp), "sc")
                                consume(grgs[0], "sc")
                                consume(grgs[1], "sc")
                                consume(grgs[0], "vc")
                                consume(grgs[1], "vc")
                                if GLEVEL >= 3:
                                    s.wait("vc")
                                    s.acq("mtb")
                                    for jj in range(8):
                                        h_ = grp * 8 + jj
                                        s.activation(mtb[:, jj * 128:(jj + 1) * 128],
                                                     mpre[:, jj * 128:(jj + 1) * 128], AF.Exp,
                                                     bias=negca[:, h_:h_ + 1])
                                    incl(s, "sc")
                                    consume("negca", "sc")
                                if GLEVEL >= 4:
                                    v.wait("sc")
                                    v.acq("gmt")
                                    v.acq("cdec")
                                    gbv = gbb.ap()[:, None, :].broadcast_to([128, 8, 128])
                                    inc(v.tensor_tensor(gmt.ap().rearrange("p (a u) -> p a u", a=8),
                                                        mtb.ap().rearrange("p (a u) -> p a u", a=8),
                                                        gbv, OP.mult), "vc")
                                    consume("mtb", "vc")
                                    cv = xcv.ap()[:, None, 17 * TT + toff:17 * TT + toff + 128] \
                                        .broadcast_to([128, 8, 128])
                                    inc(v.tensor_tensor(cdec.ap().rearrange("p (a u) -> p a u", a=8),
                                                        cv,
                                                        eca.ap().rearrange("p (a u) -> p a u", a=8),
                                                        OP.mult), "vc")
                                gmt_ready = semn["vc"]
                                for jj in range(8 if "yonly" not in SKIP else 0):
                                    h_ = grp * 8 + jj
                                    odd = h_ % 2
                                    if odd:
                                        yslot = pEF[64:128, 640:768]
                                        rgn = "B5"
                                        tp = (0, 64)
                                    else:
                                        yslot = pEF[0:64, 256:384]
                                        rgn = "B4"
                                        tp = None
                                    t.acq(rgn)
                                    t.wait("vc", gmt_ready)
                                    t.wait("sc", sbb_sc[0])
                                    t.matmul(yslot, xti[:, h_ * 64:(h_ + 1) * 64],
                                             gmt[:, jj * 128:(jj + 1) * 128],
                                             start=True, stop=False, tile_position=tp)
                                    t.matmul(yslot, Sbb[:, h_ * 64:(h_ + 1) * 64],
                                             cdec[:, jj * 128:(jj + 1) * 128],
                                             start=False, stop=True, tile_position=tp)
                                    incl(t, "tn")
                                    consume("gmt", "tn")
                                    consume("cdec", "tn")
                                    v.wait("tn")
                                    rows = slice(64, 128) if odd else slice(0, 64)
                                    ydst = yb[rows, (h_ // 2) * TT + toff:
                                              (h_ // 2) * TT + toff + 128]
                                    xsrc = xcv[rows, (h_ // 2) * TT + toff:
                                               (h_ // 2) * TT + toff + 128]
                                    inc(v.scalar_tensor_tensor(ydst, xsrc,
                                                               dbc_f[rows, h_:h_ + 1],
                                                               yslot, OP.mult, OP.add), "vc")
                                    consume(rgn, "vc")
                            y_done = semn["tn"]
                            consume("ldatm", "tn")
                            consume("xti", "tn")

                        for halfh in range(2):
                            t.acq("B2")
                            t.acq("B3")
                            t.wait("vc", xops_done)
                            t.wait("gp", xtp_gp)
                            t.wait("tn", y_done)
                            t.matmul(pCD[:, 0:512], btm[:],
                                     xtp[:, halfh * 1024: halfh * 1024 + 512],
                                     start=True, stop=True)
                            t.matmul(pCD[:, 512:1024], btm[:],
                                     xtp[:, halfh * 1024 + 512: (halfh + 1) * 1024],
                                     start=True, stop=True)
                            incl(t, "tn")
                            consume("xtp", "tn")
                            consume("btm_r", "tn")
                            v.wait("tn")
                            h0 = halfh * 16
                            dv = decbc.ap()[:, h0:h0 + 16, None].broadcast_to([128, 16, 64])
                            v.tensor_tensor(
                                mpre.ap().rearrange("p (a u) -> p a u", a=16),
                                Sf.ap().rearrange("p (a u) -> p a u", a=32)[:, h0:h0 + 16],
                                dv, OP.mult)
                            fence(v, "vc")
                            inc(v.tensor_tensor(Sf[:, h0 * 64:(h0 + 16) * 64],
                                                mpre.ap().bitcast(F32) if False else mpre[:],
                                                pCD[:], OP.add), "vc")
                            consume("B2", "vc")
                            consume("B3", "vc")
                            feed(1)
                        s.wait("vc")
                        inc(s.copy(Sbb[:], Sf[:]), "sc")
                        sbb_sc[0] = semn["sc"]

                    if own and "rs" in SKIP:
                        pass
                    elif own:
                        v.wait("sc")
                        inc(v.tensor_tensor(sz[:], yb[:], sz[:], OP.mult), "vc")
                        v.wait("vc")
                        if k == 8 and d_y is not None:
                            sy.wait("vc")
                            inc(sy.dma_start(d_y, yb[:]), "dbx", 16)
                            consume("yb", "dbx")
                        for kc in range(16):
                            sq = cacc[4 + kc % 2][:]
                            v.acq("sq%d" % (kc % 2))
                            inc(v.tensor_tensor(sq, sz[:, kc * TT:(kc + 1) * TT],
                                                sz[:, kc * TT:(kc + 1) * TT], OP.mult), "vc")
                            t.wait("vc")
                            if kc == 0:
                                t.acq("B7")
                            t.matmul(pS[0:1, 160:160 + TT], onesc_bf[:], sq,
                                     start=(kc == 0), stop=(kc == 15))
                            incl(t, "tn")
                            consume("sq%d" % (kc % 2), "tn")
                        v.wait("tn")
                        inc(v.tensor_scalar(rrow[:], pS[0:1, 160:160 + TT], 1.0 / DI, EPS,
                                            OP.mult, OP.add), "vc")
                        consume("B7", "vc")
                        s.wait("vc")
                        s.acq("rrowb")
                        s.activation(rrow[:], rrow[:], AF.Ln)
                        fence(s, "sc")
                        inc(s.activation(rrowb[:], rrow[:], AF.Exp, scale=-0.5), "sc")
                        t.wait("sc")
                        t.acq("B0")
                        t.matmul(pAB[:, 0:TT], onesr_bf[0:1, 0:128], rrowb[:],
                                 start=True, stop=True)
                        incl(t, "tn")
                        consume("rrowb", "tn")
                        v.wait("tn")
                        inc(v.tensor_copy(rsb[:], pAB[:, 0:TT]), "vc")
                        consume("B0", "vc")
                        v.wait("vc")
                        inc(v.tensor_tensor(sz.ap().rearrange("p (c u) -> p c u", c=16),
                                            sz.ap().rearrange("p (c u) -> p c u", c=16),
                                            rsb.ap()[:, None, :].broadcast_to([128, 16, TT]),
                                            OP.mult), "vc")
                        feed(2)
                        tcol = (k - 8) * TT
                        sy.wait("vc")
                        sy.wait("bg")
                        inc(sy.dma_start(
                            s_g.rearrange("p (c u) -> p c u", c=16)[:, :, tcol:tcol + TT],
                            sz.ap().rearrange("p (c u) -> p c u", c=16)), "bg", 16)
                        consume("sz", "bg")
                        if k == 8:
                            if d_zx is not None:
                                sy.wait("bg")
                                inc(sy.dma_start(d_zx, sz[:]), "dbx", 16)
                                consume("sz", "dbx")
                            if d_conv is not None:
                                inc(sy.dma_start(d_conv, xcv[:]), "dbx", 16)
                                consume("xcv", "dbx")
                            if d_xbc is not None:
                                sy.wait("dbx")
                                inc(sy.dma_start(
                                    d_xbc.rearrange("p (c u) -> p c u", c=CC),
                                    xbp_.ap().rearrange("p (c u) -> p c u",
                                                        c=CC)[:, :, 3:3 + TT]), "dbx", 16)
                                consume("xbp%d" % (k % 2), "dbx")
                    else:
                        consume("an_buf", "vc")

                    consume("an_buf", "tn")
                    consume("xcv", "tn")
                    consume("xcv", "vc")
                    if own:
                        consume("sz", "vc")

                if not thr_emitted:
                    emit_thr()
                emit_pass2(99)
                a_done = dict(semn)
                if STOP == "aphase":
                    for snx in ["tn", "vc", "sc", "gp", "o0", "o1", "qs", "bg", "dbx", "th0", "th1", "fh0", "fh1", "ws0", "ws1"]:
                        sy.wait(snx)
                    return

            with ExitStack() as tctx:
                TE_ = tctx.enter_context
                wtail = TE_(nc.sbuf_tensor("wtail", [128, 32 * 1024], BF16))
                gstr = [TE_(nc.sbuf_tensor("gstr%d" % i, [128, 16 * 128], BF16))
                        for i in range(2)]
                astr = [TE_(nc.sbuf_tensor("astr%d" % i, [128, 8 * 128], BF16))
                        for i in range(2)]
                xstr = [TE_(nc.sbuf_tensor("xstr%d" % i, [128, 1024], F32)) for i in range(2)]
                u_tm = TE_(nc.sbuf_tensor("u_tm", [128, 1024], BF16))
                crow = TE_(nc.sbuf_tensor("crow", [1, 1024], BF16))
                crowf = TE_(nc.sbuf_tensor("crowf", [1, 1024], F32))
                astg = TE_(nc.sbuf_tensor("astg", [128, 1024], F32))
                hstg = [TE_(nc.sbuf_tensor("hstg%d" % i, [128, 1024], F32)) for i in range(2)]

                barrier()

                inc(g.dma_start(wtail.ap().rearrange("p (c o) -> p c o", c=32)[:, 0:16, :],
                                w_fu[0:DI].rearrange("(c p) o -> p c o", p=128)), "wp", 16)
                v.wait("wp")
                for kc in range(16):
                    ins = v.tensor_scalar(wtail[:, kc * 1024:(kc + 1) * 1024],
                                          wtail[:, kc * 1024:(kc + 1) * 1024],
                                          gnc_f[:, kc:kc + 1], None, OP.mult)
                inc(ins, "vc")
                sy.wait("ws0")
                sy.wait("ws1")
                inc(sy.dma_start(wtail[:, 16 * 1024:24 * 1024], s_wzc[:]), "tw", 16)
                sy.wait("tw")
                inc(sy.dma_start(wtail[:, 24 * 1024:32 * 1024], s_woc[:]), "tw", 16)
                wt_vc, wt_xq = semn["vc"], semn["tw"]

                inc(v.tensor_copy(xsb[:], xsums[:]), "vc")
                t.wait("vc")
                t.wait("tw", wt_xq)
                t.acq("B4")
                t.acq("B5")
                for dc in range(8):
                    woc0 = 24 * 1024 + dc * 1024
                    t.matmul(pEF[0:1, 0:512], xsb[:, dc:dc + 1], wtail[:, woc0:woc0 + 512],
                             start=(dc == 0), stop=(dc == 7))
                    t.matmul(pEF[0:1, 512:1024], xsb[:, dc:dc + 1],
                             wtail[:, woc0 + 512:woc0 + 1024], start=(dc == 0), stop=(dc == 7))
                incl(t, "tn")
                v.wait("tn")
                v.tensor_copy(crowf[:], pEF[0:1, 0:1024])
                fence(v, "vc")
                inc(v.tensor_copy(crow[:], crowf[:]), "vc")
                consume("B4", "vc")
                consume("B5", "vc")
                crow_vc = semn["vc"]

                def tail_loads(i2):
                    bi2 = i2 % 2
                    sy.acq("gstr%d" % bi2)
                    inc(sy.dma_start(gstr[bi2].ap().rearrange("p (c u) -> p c u", c=16),
                                     s_g.rearrange("p (c u) -> p c u", c=16)
                                     [:, :, i2 * 128:(i2 + 1) * 128]), "tg%d" % bi2, 16)
                    sy.acq("astr%d" % bi2)
                    inc(sy.dma_start(astr[bi2].ap().rearrange("p (c u) -> p c u", c=8),
                                     s_an.rearrange("p (c u) -> p c u", c=8)
                                     [:, :, i2 * 128:(i2 + 1) * 128]), "ta%d" % bi2, 16)
                    sy.acq("xstr%d" % bi2)
                    inc(sy.dma_start(xstr[bi2][:], xcore[i2 * 128:(i2 + 1) * 128]),
                        "tx%d" % bi2, 16)

                a_al_map = {}

                def tail_xln(i2):
                    bi2 = i2 % 2
                    v.wait("tx%d" % bi2)
                    s.wait("tx%d" % bi2)
                    ln_stats(xstr[bi2][:], False, cb=8 * bi2)
                    a_al_map[i2] = semn["vc"]

                sy.wait("bg")
                tail_loads(0)
                tail_xln(0)
                for i in range(16):
                    bi = i % 2
                    cb = 8 * bi
                    a_al = a_al_map[i]

                    t.wait("tg%d" % bi)
                    t.wait("ta%d" % bi)
                    t.wait("tx%d" % bi)
                    t.wait("vc", wt_vc)
                    t.acq("B2")
                    t.acq("B3")
                    for dc in range(8):
                        woc0 = 24 * 1024 + dc * 1024
                        t.matmul(pCD[:, 0:512], astr[bi][:, dc * 128:(dc + 1) * 128],
                                 wtail[:, woc0:woc0 + 512], start=(dc == 0), stop=(dc == 7))
                        t.matmul(pCD[:, 512:1024], astr[bi][:, dc * 128:(dc + 1) * 128],
                                 wtail[:, woc0 + 512:woc0 + 1024],
                                 start=(dc == 0), stop=(dc == 7))
                    incl(t, "tn")
                    v.wait("tn")
                    v.acq("u_tm")
                    inc(v.tensor_copy(u_tm[:], pCD[:]), "vc")
                    consume("B2", "vc")
                    consume("B3", "vc")
                    t.wait("vc")
                    t.acq("B4")
                    t.acq("B5")
                    t.matmul(pEF[0:1, 0:512], onesc_bf[:], u_tm[:, 0:512],
                             start=True, stop=True)
                    t.matmul(pEF[0:1, 512:1024], onesc_bf[:], u_tm[:, 512:1024],
                             start=True, stop=True)
                    incl(t, "tn")

                    t.acq("B0")
                    t.acq("B1")
                    t.wait("vc", crow_vc)
                    t.matmul(pAB[:, 0:512], onesr_bf[0:1, 0:128], crow[0:1, 0:512],
                             start=True, stop=False)
                    t.matmul(pAB[:, 512:1024], onesr_bf[0:1, 0:128], crow[0:1, 512:1024],
                             start=True, stop=False)
                    carry_bcast_tn = semn["tn"] + 1
                    t.matmul(pAB[:, 0:512], ut_bf[:], u_tm[:, 0:512], start=False, stop=False)
                    t.matmul(pAB[:, 512:1024], ut_bf[:], u_tm[:, 512:1024],
                             start=False, stop=False)
                    incl(t, "tn")
                    consume("u_tm", "tn")
                    for dc in range(8):
                        wzc0 = 16 * 1024 + dc * 1024
                        t.matmul(pAB[:, 0:512], astr[bi][:, dc * 128:(dc + 1) * 128],
                                 wtail[:, wzc0:wzc0 + 512], start=False, stop=False)
                        t.matmul(pAB[:, 512:1024], astr[bi][:, dc * 128:(dc + 1) * 128],
                                 wtail[:, wzc0 + 512:wzc0 + 1024], start=False, stop=False)
                    for kc in range(16):
                        t.matmul(pAB[:, 0:512], gstr[bi][:, kc * 128:(kc + 1) * 128],
                                 wtail[:, kc * 1024:kc * 1024 + 512],
                                 start=False, stop=(kc == 15))
                        t.matmul(pAB[:, 512:1024], gstr[bi][:, kc * 128:(kc + 1) * 128],
                                 wtail[:, kc * 1024 + 512:(kc + 1) * 1024],
                                 start=False, stop=(kc == 15))
                    incl(t, "tn")
                    consume("gstr%d" % bi, "tn")
                    consume("astr%d" % bi, "tn")

                    if i + 1 < 16:
                        tail_loads(i + 1)
                        tail_xln(i + 1)

                    v.wait("tn")
                    v.tensor_tensor(crowf[:], crowf[:], pEF[0:1, 0:1024], OP.add)
                    fence(v, "vc")
                    inc(v.tensor_copy(crow[:], crowf[:]), "vc")
                    consume("B4", "vc")
                    consume("B5", "vc")
                    crow_vc = semn["vc"]

                    v.wait("vc", a_al)
                    inc(v.scalar_tensor_tensor(astg[:], xstr[bi][:], stat[:, cb + 5:cb + 6],
                                               pAB[:], OP.mult, OP.add), "vc")
                    consume("B0", "vc")
                    consume("B1", "vc")
                    v.wait("vc")
                    ln_stats(astg[:], False, cb=cb)
                    v.acq("hstg%d" % bi)
                    v.scalar_tensor_tensor(hstg[bi][:], astg[:], stat[:, cb + 5:cb + 6],
                                           xstr[bi][:], OP.mult, OP.add)
                    fence(v, "vc")
                    inc(v.tensor_scalar(hstg[bi][:], hstg[bi][:], stat[:, cb + 6:cb + 7],
                                        None, OP.add), "vc")
                    consume("xstr%d" % bi, "vc")
                    consume("xstr%d" % bi, "sc")
                    sy.wait("vc")
                    inc(sy.dma_start(s_h1[i], hstg[bi][:]), "th%d" % bi, 16)
                    consume("hstg%d" % bi, "th%d" % bi)

            if STOP == "tail":
                for snx in ["tn", "vc", "sc", "gp", "o0", "o1", "qs", "bg", "dbx", "th0", "th1", "fh0", "fh1", "ws0", "ws1"]:
                    sy.wait(snx)
                return

            with ExitStack() as fctx:
                FE = fctx.enter_context
                wq1 = FE(nc.sbuf_tensor("wq1", [128, 8 * DFF], BF16))
                h1t = [FE(nc.sbuf_tensor("h1t%d" % i, [128, 2048], F32)) for i in range(2)]
                fint = FE(nc.sbuf_tensor("fint", [128, 1024], BF16))
                finfs = [FE(nc.sbuf_tensor("finf%d" % i, [128, 8 * TT], BF16))
                         for i in range(2)]
                hhb = [FE(nc.sbuf_tensor("hhb%d" % i, [128, 32 * TT], BF16)) for i in range(2)]

                barrier()
                sy.wait("qs")
                inc(sy.dma_start(wq1[:], s_wq1[:]), "fw", 16)
                wq1_ready = semn["fw"]

                fin_ready_map = {}

                def f1_stage(k2):
                    bi2 = k2 % 2
                    sy.acq("h1t%d" % bi2)
                    inc(sy.dma_start(h1t[bi2].ap().rearrange("p (s d) -> p s d", s=2),
                                     s_h1[2 * k2:2 * k2 + 2].transpose([1, 0, 2])),
                        "f%d" % bi2, 16)
                    for j in range(2):
                        v.wait("f%d" % bi2)
                        s.wait("f%d" % bi2)
                        h1sub = h1t[bi2][:, j * 1024:(j + 1) * 1024]
                        ln_stats(h1sub, False)
                        v.acq("fint")
                        inc(v.tensor_scalar(fint[:], h1sub, stat[:, 5:6], stat[:, 6:7],
                                            OP.mult, OP.add), "vc")
                        t.wait("vc")
                        t.acq("B6")
                        for dc in range(8):
                            t.transpose(pT[:, dc * 128:(dc + 1) * 128],
                                        fint[:, dc * 128:(dc + 1) * 128], id_bf[:])
                        incl(t, "tn")
                        consume("fint", "tn")
                        v.wait("tn")
                        v.acq("finf%d" % bi2)
                        fv = finfs[bi2].ap().rearrange("p (c s u) -> p c s u", c=8, s=2)
                        inc(v.tensor_copy(fv[:, :, j, :],
                                          pT.ap().rearrange("p (c u) -> p c u", c=8)), "vc")
                        consume("B6", "vc")
                    fin_ready_map[k2] = semn["vc"]
                    consume("h1t%d" % bi2, "vc")
                    consume("h1t%d" % bi2, "sc")

                sy.wait("th0")
                sy.wait("th1")
                f1_stage(0)
                for k in range(8):
                    bi = k % 2
                    t.wait("fw", wq1_ready)
                    t.wait("vc", fin_ready_map[k])
                    for mc in range(32):
                        ps, off, rg = mslots[mc % 4]
                        t.acq(rg)
                        for dc in range(8):
                            t.matmul(ps[:, off:off + TT],
                                     wq1[:, dc * DFF + mc * 128: dc * DFF + (mc + 1) * 128],
                                     finfs[bi][:, dc * TT:(dc + 1) * TT],
                                     start=(dc == 0), stop=(dc == 7))
                        incl(t, "tn")
                        s.wait("tn")
                        if mc == 0:
                            s.acq("hh%d" % bi)
                        inc(s.activation(hhb[bi][:, mc * TT:(mc + 1) * TT], ps[:, off:off + TT],
                                         AF.Relu, bias=b1c_f[:, mc:mc + 1]), "sc")
                        consume(rg, "sc")
                    consume("finf%d" % bi, "tn")
                    hh_sc = semn["sc"]
                    if k + 1 < 8:
                        f1_stage(k + 1)
                    sy.wait("sc", hh_sc)
                    inc(sy.dma_start(
                        s_hh.rearrange("p (c u) -> p c u", c=32)[:, :, k * TT:(k + 1) * TT],
                        hhb[bi].ap().rearrange("p (c u) -> p c u", c=32)), "fh%d" % bi, 16)
                    consume("hh%d" % bi, "fh%d" % bi)

            if STOP == "ffn1":
                for snx in ["tn", "vc", "sc", "gp", "o0", "o1", "qs", "bg", "dbx", "th0", "th1", "fh0", "fh1", "ws0", "ws1"]:
                    sy.wait(snx)
                return

            with ExitStack() as f2ctx:
                FE2 = f2ctx.enter_context
                wq2 = FE2(nc.sbuf_tensor("wq2", [128, 32 * D], BF16))
                hhs = [FE2(nc.sbuf_tensor("hhs%d" % i, [128, 32 * 128], BF16))
                       for i in range(2)]
                h1s = [FE2(nc.sbuf_tensor("h1s%d" % i, [128, 1024], F32)) for i in range(2)]
                fout = FE2(nc.sbuf_tensor("fout", [128, 1024], F32))
                h2st = [FE2(nc.sbuf_tensor("h2st%d" % i, [128, 1024], F32)) for i in range(2)]

                barrier()
                inc(sy.dma_start(wq2[:], s_wq2[:]), "fw", 16)
                wq2_ready = semn["fw"]

                def f2_loads(i2):
                    bi2 = i2 % 2
                    sy.acq("hhs%d" % bi2)
                    inc(sy.dma_start(hhs[bi2].ap().rearrange("p (c u) -> p c u", c=32),
                                     s_hh.rearrange("p (c u) -> p c u", c=32)
                                     [:, :, i2 * 128:(i2 + 1) * 128]), "g2%d" % bi2, 16)
                    sy.acq("h1s%d" % bi2)
                    inc(sy.dma_start(h1s[bi2][:], s_h1[i2]), "h2%d" % bi2, 16)

                sy.wait("fh0")
                sy.wait("fh1")
                f2_loads(0)
                for i in range(16):
                    bi = i % 2
                    ps = pAB if bi == 0 else pCD
                    rgs = ("B0", "B1") if bi == 0 else ("B2", "B3")
                    t.wait("fw", wq2_ready)
                    t.wait("g2%d" % bi)
                    t.acq(rgs[0])
                    t.acq(rgs[1])
                    for fc in range(32):
                        t.matmul(ps[:, 0:512], hhs[bi][:, fc * 128:(fc + 1) * 128],
                                 wq2[:, fc * 1024:fc * 1024 + 512],
                                 start=(fc == 0), stop=(fc == 31))
                        t.matmul(ps[:, 512:1024], hhs[bi][:, fc * 128:(fc + 1) * 128],
                                 wq2[:, fc * 1024 + 512:(fc + 1) * 1024],
                                 start=(fc == 0), stop=(fc == 31))
                    incl(t, "tn")
                    consume("hhs%d" % bi, "tn")
                    if i + 1 < 16:
                        f2_loads(i + 1)
                    v.wait("tn")
                    v.wait("h2%d" % bi)
                    inc(v.tensor_tensor(fout[:], ps[:], b2bc_f[:], OP.add), "vc")
                    consume(rgs[0], "vc")
                    consume(rgs[1], "vc")
                    v.wait("vc")
                    ln_stats(fout[:], False)
                    v.acq("h2st%d" % bi)
                    v.scalar_tensor_tensor(h2st[bi][:], fout[:], stat[:, 5:6], h1s[bi][:],
                                           OP.mult, OP.add)
                    fence(v, "vc")
                    inc(v.tensor_scalar(h2st[bi][:], h2st[bi][:], stat[:, 6:7], None, OP.add),
                        "vc")
                    consume("h1s%d" % bi, "vc")
                    sy.wait("vc")
                    inc(sy.dma_start(out[i * 128:(i + 1) * 128], h2st[bi][:]), "o%d" % bi, 16)
                    consume("h2st%d" % bi, "o%d" % bi)

            for snx in ["tn", "vc", "sc", "gp", "o0", "o1", "qs", "bg", "dbx", "th0", "th1", "fh0", "fh1", "ws0", "ws1"]:
                sy.wait(snx)

    return nc


LAST_RES = None
_NC_CACHE = {}


def _get_nc():
    if "nc" not in _NC_CACHE:
        _NC_CACHE["nc"] = build_nc()
    return _NC_CACHE["nc"]


def kernel(**inputs):
    x = np.ascontiguousarray(np.asarray(inputs["L0_attn_x"], dtype=np.float32))
    normw = np.asarray(inputs["L0_attn_norm_w"], dtype=np.float32)
    pj = np.ascontiguousarray(np.asarray(inputs["L0_attn_in_proj_W"], dtype=np.float32))
    cw = np.asarray(inputs["L0_attn_conv_W"], dtype=np.float32)
    cb = np.asarray(inputs["L0_attn_conv_b"], dtype=np.float32)
    dtb = np.asarray(inputs["L0_attn_dt_bias"], dtype=np.float32)
    alog = np.asarray(inputs["L0_attn_A_log"], dtype=np.float32)
    Dv = np.asarray(inputs["L0_attn_D"], dtype=np.float32)
    gnw = np.asarray(inputs["L0_attn_gnorm_w"], dtype=np.float32)
    zw = np.asarray(inputs["L0_attn_zero_W"], dtype=np.float32)
    ow = np.asarray(inputs["L0_attn_one_W"], dtype=np.float32)
    fw = np.ascontiguousarray(np.asarray(inputs["L0_attn_fusion_W"], dtype=np.float32))
    w1 = np.asarray(inputs["L0_ffn_W1"], dtype=np.float32)
    b1 = np.asarray(inputs["L0_ffn_b1"], dtype=np.float32)
    w2 = np.asarray(inputs["L0_ffn_W2"], dtype=np.float32)
    b2 = np.asarray(inputs["L0_ffn_b2"], dtype=np.float32)

    consts = dict(
        w_pj=pj,
        w_zt=np.ascontiguousarray(zw.T),
        w_ot=np.ascontiguousarray(ow.T),
        w_fu=fw,
        w_q1t=np.ascontiguousarray(w1.T),
        w_q2t=np.ascontiguousarray(w2.T),
        c_ut=(np.arange(128)[:, None] <= np.arange(128)[None, :]).astype(np.float32),
        c_sl=np.where(np.arange(128)[:, None] > np.arange(128)[None, :],
                      np.float32(NEG), np.float32(0.0)).astype(np.float32),
        c_id=np.eye(128, dtype=np.float32),
        c_onesc=np.ones((128, 1), np.float32),
        c_onesr=np.ones((1, 512), np.float32),
        c_sel=(np.arange(128)[:, None] == 127).astype(np.float32),
        c_cwt=np.ascontiguousarray(cw.T),
        c_cb=np.ascontiguousarray(cb.reshape(CC, 128).T),
        c_dtb=np.ascontiguousarray(dtb.reshape(32, 1)),
        c_alog=np.ascontiguousarray(alog.reshape(32, 1)),
        c_dbc=np.ascontiguousarray(np.tile(Dv[None, :], (128, 1))),
        c_b1c=np.ascontiguousarray(b1.reshape(32, 128).T),
        c_b2bc=np.ascontiguousarray(np.tile(b2[None, :], (128, 1))),
        c_gnc=np.ascontiguousarray(gnw.reshape(16, 128).T),
        c_nwc=np.ascontiguousarray(normw.reshape(8, 128).T),
    )

    in_maps = []
    zeros = np.zeros((TPC, D), np.float32)
    for c in range(8):
        b, half = c // 2, c % 2
        m = dict(consts)
        m["xcore"] = np.ascontiguousarray(x[b, half * TPC:(half + 1) * TPC])
        m["xprev"] = np.ascontiguousarray(x[b, 0:TPC]) if half else zeros
        m["gate"] = np.full((128, 1), np.float32(half))
        in_maps.append(m)

    nc = _get_nc()
    res = run_bass_kernel_spmd(nc, in_maps[:NCORES], core_ids=list(range(NCORES)))
    o = np.zeros((B, L, D), np.float32)
    for c in range(NCORES):
        b, half = c // 2, c % 2
        o[b, half * TPC:(half + 1) * TPC] = np.asarray(res.results[c]["out"])
    if DBG:
        global LAST_RES
        LAST_RES = res
    return o
